# version 01 (final)
# baseline (speedup 1.0000x reference)
import contextlib
import numpy as np
import ml_dtypes
import concourse.bass as bass
import concourse.mybir as mybir
from concourse.bass_utils import run_bass_kernel_spmd
from concourse.alu_op_type import AluOpType

F32 = mybir.dt.float32
BF16 = mybir.dt.bfloat16
I32 = mybir.dt.int32
AF = mybir.ActivationFunctionType
AX = mybir.AxisListType
MULT = AluOpType.mult
ADD = AluOpType.add
SUB = AluOpType.subtract
MAXOP = AluOpType.max
ISEQ = AluOpType.is_equal

L = 4096
D = 2048
KC = D // 128
INNER = 4096
NG = 32
import os as _os
NL = int(_os.environ.get("KNL", "32"))
EPS = 1e-5


def _t(a):
    return np.ascontiguousarray(np.asarray(a).T)


def host_prep(inputs):
    d = {}
    d["w_in_T"] = _t(inputs["L0_attn_in_proj_w"])
    d["w_gate_T"] = _t(inputs["L0_attn_gate_proj_w"])
    d["w_out_T"] = _t(inputs["L0_attn_out_proj_w"])
    cw = np.asarray(inputs["L0_attn_conv_weight"])
    d["conv_w_blk"] = np.ascontiguousarray(
        cw.reshape(NG, 128, 4).transpose(1, 0, 2))
    d["conv_b_blk"] = np.ascontiguousarray(
        np.asarray(inputs["L0_attn_conv_bias"]).reshape(NG, 128).T)
    for nm, key in (("C_blk", "L0_attn_C"), ("D_blk", "L0_attn_D"),
                    ("B_blk_flat", "L0_attn_B")):
        d[nm] = np.ascontiguousarray(
            np.asarray(inputs[key]).reshape(-1).reshape(NG, 128).T)
    dtw = np.asarray(inputs["L0_attn_dt_w"]).reshape(-1)
    blk = np.zeros((128, NG, 2), np.float32)
    for g in range(NG):
        blk[0:64, g, 0] = dtw[128 * g: 128 * g + 64]
        blk[64:128, g, 1] = dtw[128 * g + 64: 128 * g + 128]
    d["dtw_blk"] = blk
    d["dt_b2"] = np.ascontiguousarray(
        np.asarray(inputs["L0_attn_dt_b"]).reshape(NG, 2).T)
    d["Alog_pp"] = np.ascontiguousarray(
        np.asarray(inputs["L0_attn_A_log"]).reshape(-1).reshape(NG, 128).T)
    bc2 = np.zeros((2, 128), np.float32)
    bc2[0, 0:64] = 1.0
    bc2[1, 64:128] = 1.0
    d["bc2"] = bc2
    d["cls_T"] = _t(inputs["L0_ffn_classifier_w"])
    d["ebias"] = np.asarray(inputs["L0_ffn_e_score_correction_bias"]).reshape(1, 12)
    d["moe_gate_T"] = np.ascontiguousarray(
        np.asarray(inputs["L0_ffn_gate_w"]).transpose(0, 2, 1))
    d["moe_up_T"] = np.ascontiguousarray(
        np.asarray(inputs["L0_ffn_up_w"]).transpose(0, 2, 1))
    d["moe_down_T"] = np.ascontiguousarray(
        np.asarray(inputs["L0_ffn_down_w"]).transpose(0, 2, 1))
    for nm in ("Wq", "Wk", "Wv", "Wo"):
        d[nm + "_T"] = _t(inputs["L1_attn_" + nm])
    d["gq_T"] = _t(inputs["L1_ffn_gate_w"])
    d["uq_T"] = _t(inputs["L1_ffn_up_w"])
    d["dq_T"] = _t(inputs["L1_ffn_down_w"])
    d["gsc_blk"] = np.ascontiguousarray(
        np.asarray(inputs["L1_ffn_gate_scale"]).reshape(64, 128).T)
    d["usc_blk"] = np.ascontiguousarray(
        np.asarray(inputs["L1_ffn_up_scale"]).reshape(64, 128).T)
    d["dsc_blk"] = np.ascontiguousarray(
        np.asarray(inputs["L1_ffn_down_scale"]).reshape(16, 128).T)
    d["ident"] = np.eye(128, dtype=np.float32)
    inv = 1.0 / (10000.0 ** (np.arange(0, 128, 2, dtype=np.float64) / 128.0))
    t = np.arange(L, dtype=np.float64)
    fr = np.outer(inv, t)
    cos = np.cos(fr); sin = np.sin(fr)
    cos2 = np.repeat(cos, 2, axis=0)
    sin2 = np.empty((128, L), np.float64)
    sin2[0::2] = -sin
    sin2[1::2] = sin
    d["rope_cos"] = cos2.astype(np.float32)
    d["rope_sin"] = sin2.astype(np.float32)
    psw = np.zeros((128, 128), np.float32)
    for i in range(64):
        psw[2 * i, 2 * i + 1] = 1.0
        psw[2 * i + 1, 2 * i] = 1.0
    d["pswap"] = psw
    masks = np.zeros((4, 128, 512), np.float32)
    jj = np.arange(128)[:, None]
    qq = np.arange(512)[None, :]
    for r in range(4):
        masks[r] = (128 * r + jj <= qq).astype(np.float32)
    d["masks"] = masks
    return d



class Sem:
    def __init__(self, ctx, nc, name):
        self.h = ctx.enter_context(nc.semaphore(name))
        self.n = 0


class Ch:

    def __init__(self, p, name):
        self.s = p.sem(name)
        self.eng = None
        self.marks = {}
        self.record = False

    @property
    def n(self):
        return self.s.n

    def bind(self, eng, record=False):
        self.eng = eng
        self.marks = {}
        self.record = record
        return self

    def w(self, src, n=None):
        if self.record:
            return
        n = src.n if n is None else n
        if n <= 0:
            return
        h = src.s.h if isinstance(src, Ch) else src.h
        if self.marks.get(h, -1) >= n:
            return
        self.eng.wait_ge(h, n)
        self.marks[h] = n

    def do(self, f, selfsync=True):
        if self.record:
            self.s.n += 1
            return self.s.n
        if selfsync and self.s.n > 0 and self.marks.get(self.s.h, -1) < self.s.n:
            self.eng.wait_ge(self.s.h, self.s.n)
            self.marks[self.s.h] = self.s.n
        inst = f()
        inst.then_inc(self.s.h, 1)
        self.s.n += 1
        return self.s.n


class _MockEng:
    def __getattr__(self, nm):
        raise RuntimeError(f"mock engine method {nm} called in record pass")


class P:

    def __init__(self, nc, ctx, block):
        self.nc = nc
        self.ctx = ctx
        self.block = block
        self.t = {}
        self.sb = {}
        self.sems = {}
        self._consts = {}
        self._const_tile = self.ctx.enter_context(
            self.nc.sbuf_tensor("const_pool", [128, 32], F32))
        self._const_n = 0
        self._consts_emitted = False
        self.record = False
        self.ve = Ch(self, "ch_v")
        self.se = Ch(self, "ch_s")
        self.te = Ch(self, "ch_t")
        self.ge = Ch(self, "ch_g")

    def sem(self, name):
        s = Sem(self.ctx, self.nc, name)
        self.sems[name] = s
        return s

    def constf(self, value):
        if value in self._consts:
            return self._consts[value]
        ap = self._const_tile[:, self._const_n:self._const_n + 1]
        self._const_n += 1
        self.nc.const_aps.register(F32, float(value), ap)
        self._consts[value] = ap
        return ap

    def emit_consts(self, v):
        for val, ap in self._consts.items():
            self.ve.do(lambda val=val, ap=ap: v.memset(ap, float(val)))
        return self.ve.n

    def dma(self, q, out, in_, sem, k=16):
        if self.record:
            sem.n += k
            return sem.n
        q.dma_start(out=out, in_=in_).then_inc(sem.h, k)
        sem.n += k
        return sem.n

    def qwait(self, q, src, n=None):
        if self.record:
            return
        n = src.n if n is None else n
        if n <= 0:
            return
        h = src.s.h if isinstance(src, Ch) else src.h
        q.wait_ge(h, n)

    def two_pass(self, bodies):
        snap = {nm: s.n for nm, s in self.sems.items()}
        mock = _MockEng()
        self.record = True
        for ch in (self.ve, self.se, self.te, self.ge):
            ch.bind(mock, record=True)
        order = ["vector", "scalar", "tensor", "gpsimd", "sync"]
        for nm in order:
            if nm in bodies:
                bodies[nm](mock)
        self.record = False
        for nm, s in self.sems.items():
            s.n = snap[nm]
        chmap = {"vector": self.ve, "scalar": self.se,
                 "tensor": self.te, "gpsimd": self.ge}
        for nm in order:
            if nm not in bodies:
                continue
            fn = bodies[nm]
            deco = getattr(self.block, nm)

            def make(fn=fn, nm=nm):
                def body(eng):
                    if nm in chmap:
                        chmap[nm].bind(eng)
                    fn(eng)
                return body
            deco(make())


DRAM_SPECS = {
    "x_in": ([L, D], F32, "in"),
    "w_in_T": ([D, INNER], F32, "in"),
    "w_gate_T": ([D, INNER], F32, "in"),
    "w_out_T": ([INNER, D], F32, "in"),
    "conv_w_blk": ([128, NG, 4], F32, "in"),
    "conv_b_blk": ([128, NG], F32, "in"),
    "C_blk": ([128, NG], F32, "in"),
    "D_blk": ([128, NG], F32, "in"),
    "B_blk_flat": ([128, NG], F32, "in"),
    "dtw_blk": ([128, NG, 2], F32, "in"),
    "dt_b2": ([2, NG], F32, "in"),
    "Alog_pp": ([128, NG], F32, "in"),
    "bc2": ([2, 128], F32, "in"),
    "cls_T": ([D, 12], F32, "in"),
    "ebias": ([1, 12], F32, "in"),
    "moe_gate_T": ([8, D, 1024], F32, "in"),
    "moe_up_T": ([8, D, 1024], F32, "in"),
    "moe_down_T": ([8, 1024, D], F32, "in"),
    "Wq_T": ([D, D], F32, "in"),
    "Wk_T": ([D, D], F32, "in"),
    "Wv_T": ([D, D], F32, "in"),
    "Wo_T": ([D, D], F32, "in"),
    "gq_T": ([D, 8192], I32, "in"),
    "uq_T": ([D, 8192], I32, "in"),
    "dq_T": ([8192, D], I32, "in"),
    "gsc_blk": ([128, 64], F32, "in"),
    "usc_blk": ([128, 64], F32, "in"),
    "dsc_blk": ([128, 16], F32, "in"),
    "ident": ([128, 128], F32, "in"),
    "rope_cos": ([128, L], F32, "in"),
    "rope_sin": ([128, L], F32, "in"),
    "pswap": ([128, 128], F32, "in"),
    "masks": ([4, 128, 512], F32, "in"),
    "xT_scr": ([KC, 128, L], F32, "scratch"),
    "xn_scr": ([KC, 128, L], BF16, "scratch"),
    "comb_scr": ([NG, 128, L], BF16, "scratch"),
    "attn0_scr": ([KC, 128, L], BF16, "scratch"),
    "ffn0_scr": ([KC, 128, L], BF16, "scratch"),
    "wdbg": ([8, 9, 512], F32, "scratch"),
    "wdump": ([32, 128, 13], F32, "scratch"),
    "x1_scr": ([KC, 128, L], F32, "scratch"),
    "xn1_scr": ([KC, 128, L], BF16, "scratch"),
    "q_scr": ([16, 128, L], BF16, "scratch"),
    "k_scr": ([16, 128, L], BF16, "scratch"),
    "v_scr": ([NL, 128, 16, 128], BF16, "scratch"),
    "av_scr": ([16, 128, L], BF16, "scratch"),
    "o1_scr": ([KC, 128, L], BF16, "scratch"),
    "ffn1_scr": ([KC, 128, L], BF16, "scratch"),
    "moe_g16": ([8, 8, 128, KC, 128], BF16, "scratch"),
    "moe_u16": ([8, 8, 128, KC, 128], BF16, "scratch"),
    "moe_d16": ([16, 128, 64, 128], BF16, "scratch"),
    "l1g16": ([64, 128, KC, 128], BF16, "scratch"),
    "l1u16": ([64, 128, KC, 128], BF16, "scratch"),
    "l1d16": ([16, 128, 64, 128], BF16, "scratch"),
    "y_out": ([L, D], F32, "out"),
}


def declare_dram(nc, names, ext_scratch_in=(), ext_scratch_out=()):
    t = {}
    for nm in names:
        shape, dt, kind = DRAM_SPECS[nm]
        if kind == "in":
            k = "ExternalInput"
        elif kind == "out":
            k = "ExternalOutput"
        else:
            if nm in ext_scratch_out:
                k = "ExternalOutput"
            elif nm in ext_scratch_in:
                k = "ExternalInput"
            else:
                k = "Internal"
        t[nm] = nc.dram_tensor(nm, shape, dt, kind=k).ap()
    return t



def emit_p0(p):
    nc, t = p.nc, p.t
    ve, se, te = p.ve, p.se, p.te
    idld = p.sem("p0_idld")
    lds = [p.sem("p0_ld0"), p.sem("p0_ld1")]
    stws = [p.sem("p0_stw0"), p.sem("p0_stw1")]
    NB = NL // 4

    m_xn = [0] * NL
    m_sq = [0] * NL
    m_sumsq = [0] * NL
    m_var = [0] * NL
    m_rstd = [0] * NL
    m_t32 = [[0] * 4 for _ in range(NL)]
    m_t16 = [[0] * 4 for _ in range(NL)]
    m_e32 = [[0] * 4 for _ in range(NL)]
    m_e16 = [[0] * 4 for _ in range(NL)]
    m_id16 = [0]

    with (
        nc.sbuf_tensor("p0_xt0", [128, D], F32) as xt0,
        nc.sbuf_tensor("p0_xt1", [128, D], F32) as xt1,
        nc.sbuf_tensor("p0_sq", [128, D], F32) as sq,
        nc.sbuf_tensor("p0_xn0", [128, D], BF16) as xn0,
        nc.sbuf_tensor("p0_xn1", [128, D], BF16) as xn1,
        nc.sbuf_tensor("p0_stat", [128, 8], F32) as stat,
        nc.sbuf_tensor("p0_sgx0", [128, KC, 512], F32) as sgx0,
        nc.sbuf_tensor("p0_sgx1", [128, KC, 512], F32) as sgx1,
        nc.sbuf_tensor("p0_sgn0", [128, KC, 512], BF16) as sgn0,
        nc.sbuf_tensor("p0_sgn1", [128, KC, 512], BF16) as sgn1,
        nc.sbuf_tensor("p0_id32", [128, 128], F32) as id32,
        nc.sbuf_tensor("p0_id16", [128, 128], BF16) as id16,
        nc.psum_tensor("p0_tr32", [128, 512], F32) as tr32,
        nc.psum_tensor("p0_tr16", [128, 1024], BF16) as tr16,
    ):
        xts = [xt0, xt1]
        xns = [xn0, xn1]
        sgxs = [sgx0, sgx1]
        sgns = [sgn0, sgn1]

        def vbody(v):
            ve.w(idld, 16)
            m_id16[0] = ve.do(lambda: v.tensor_copy(out=id16[:], in_=id32[:]))
            p.emit_consts(v)
            for lt in range(NL):
                lb = lt // 4
                ve.w(lds[lt % 2], 16 * (lt // 2 + 1))
                ve.do(lambda lt=lt: v.reduce_sum(
                    out=stat[:, 0:1], in_=xts[lt % 2][:], axis=AX.X))
                ve.do(lambda: v.tensor_scalar(
                    out=stat[:, 1:2], in0=stat[:, 0:1],
                    scalar1=-1.0 / D, scalar2=None, op0=MULT))
                ve.w(se, m_sq[lt])
                m_sumsq[lt] = ve.do(lambda: v.reduce_sum(
                    out=stat[:, 2:3], in_=sq[:], axis=AX.X))
                ve.do(lambda: v.tensor_scalar(
                    out=stat[:, 3:4], in0=stat[:, 2:3],
                    scalar1=1.0 / D, scalar2=EPS, op0=MULT, op1=ADD))
                ve.do(lambda: v.tensor_tensor(
                    out=stat[:, 4:5], in0=stat[:, 1:2], in1=stat[:, 1:2], op=MULT))
                m_var[lt] = ve.do(lambda: v.tensor_tensor(
                    out=stat[:, 5:6], in0=stat[:, 3:4], in1=stat[:, 4:5], op=SUB))
                ve.w(se, m_rstd[lt])
                ve.do(lambda: v.reciprocal(out=stat[:, 7:8], in_=stat[:, 6:7]))
                m_xn[lt] = ve.do(lambda lt=lt: v.tensor_scalar(
                    out=xns[lt % 2][:], in0=xts[lt % 2][:],
                    scalar1=stat[:, 1:2], scalar2=stat[:, 7:8],
                    op0=ADD, op1=MULT))
                if lt % 4 == 0 and lb >= 2:
                    ve.w(stws[lb % 2], 16 * 32 * ((lb - lb % 2) // 2))
                for c4 in range(4):
                    ve.w(te, m_t32[lt][c4])
                    m_e32[lt][c4] = ve.do(lambda lt=lt, c4=c4, lb=lb: v.tensor_copy(
                        out=sgxs[lb % 2][:, 4 * c4:4 * c4 + 4,
                                         128 * (lt % 4):128 * (lt % 4) + 128],
                        in_=tr32[:].rearrange("p (a b) -> p a b", a=4)))

        def sbody(s):
            for lt in range(NL):
                lb = lt // 4
                se.w(lds[lt % 2], 16 * (lt // 2 + 1))
                if lt >= 1:
                    se.w(ve, m_sumsq[lt - 1])
                m_sq[lt] = se.do(lambda lt=lt: s.activation(
                    out=sq[:], in_=xts[lt % 2][:], func=AF.Square))
                se.w(ve, m_var[lt])
                m_rstd[lt] = se.do(lambda: s.activation(
                    out=stat[:, 6:7], in_=stat[:, 5:6], func=AF.Sqrt))
                if lt % 4 == 0 and lb >= 2:
                    se.w(stws[lb % 2], 16 * 32 * ((lb - lb % 2) // 2))
                for c4 in range(4):
                    se.w(te, m_t16[lt][c4])
                    m_e16[lt][c4] = se.do(lambda lt=lt, c4=c4, lb=lb: s.copy(
                        out=sgns[lb % 2][:, 4 * c4:4 * c4 + 4,
                                         128 * (lt % 4):128 * (lt % 4) + 128],
                        in_=tr16[:, 0:512].rearrange("p (a b) -> p a b", a=4)))

        def tbody(tn):
            te.w(ve, m_id16[0])
            for lt in range(NL):
                te.w(lds[lt % 2], 16 * (lt // 2 + 1))
                for c4 in range(4):
                    if lt + c4 > 0:
                        te.w(ve, m_e32[lt][c4 - 1] if c4 > 0 else m_e32[lt - 1][3])
                    for j in range(4):
                        kk = 4 * c4 + j
                        cnt = te.do(lambda lt=lt, j=j, kk=kk: tn.transpose(
                            tr32[:, 128 * j:128 * (j + 1)],
                            xts[lt % 2][:, 128 * kk:128 * kk + 128],
                            id32[:]), selfsync=False)
                    m_t32[lt][c4] = cnt
                te.w(ve, m_xn[lt])
                for c4 in range(4):
                    if lt + c4 > 0:
                        te.w(se, m_e16[lt][c4 - 1] if c4 > 0 else m_e16[lt - 1][3])
                    for j in range(4):
                        kk = 4 * c4 + j
                        cnt = te.do(lambda lt=lt, j=j, kk=kk: tn.transpose(
                            tr16[:, 128 * j:128 * (j + 1)],
                            xns[lt % 2][:, 128 * kk:128 * kk + 128],
                            id16[:]), selfsync=False)
                    m_t16[lt][c4] = cnt

        def syncbody(sync):
            p.dma(sync, id32[:], t["ident"][:], idld)
            for lt in range(NL):
                if lt >= 2:
                    p.qwait(sync, te, m_t32[lt - 2][3])
                    p.qwait(sync, se, m_sq[lt - 2])
                    p.qwait(sync, ve, m_xn[lt - 2])
                p.dma(sync, xts[lt % 2][:],
                      t["x_in"][128 * lt:128 * (lt + 1), :], lds[lt % 2])
        def gbody(g):
            for lb in range(NB):
                p.qwait(g, ve, m_e32[4 * lb + 3][3])
                p.qwait(g, se, m_e16[4 * lb + 3][3])
                sg_x, sg_n = sgxs[lb % 2], sgns[lb % 2]
                for k in range(KC):
                    p.dma(g, t["xT_scr"][k, :, 512 * lb:512 * (lb + 1)],
                          sg_x[:, k, :], stws[lb % 2])
                    p.dma(g, t["xn_scr"][k, :, 512 * lb:512 * (lb + 1)],
                          sg_n[:, k, :], stws[lb % 2])
            for q in range(min(2, NB)):
                p.qwait(g, stws[q], stws[q].n)

        p.two_pass({"vector": vbody, "scalar": sbody, "tensor": tbody,
                    "gpsimd": gbody, "sync": syncbody})
    return



NG_RUN = int(_os.environ.get("KNG", "32"))
P2OFF = set(_os.environ.get("P2OFF", "").split(","))
NSEG = 4
HL = L // NSEG
NCH = HL // 512


def emit_p2(p):
    nc, t = p.nc, p.t
    ve, se, te = p.ve, p.se, p.te
    cld = p.sem("p2_cld")
    cld16 = p.sem("p2_cld16")
    xnl = [p.sem("p2_xnl%d" % i) for i in range(NSEG)]
    wld = [p.sem("p2_wld0"), p.sem("p2_wld1")]
    stc = [p.sem("p2_stc0"), p.sem("p2_stc1")]
    NGR = NG_RUN
    SL = NSEG * NGR

    def sidx(S, g):
        return S * NGR + g

    zz = lambda: [[0] * NCH for _ in range(SL)]
    m_pevac = zz(); m_gevac = zz(); m_aevac = zz(); m_uevac = zz()
    m_dtevq = zz(); m_mmin = zz(); m_mmgt = zz(); m_mmdt = zz(); m_mmbc = zz()
    m_conv = [0] * SL; m_X = [0] * SL; m_X16 = [0] * SL
    m_dtl = [0] * SL; m_dtg = [0] * SL; m_scan = [0] * SL
    m_dtexp = [0] * SL
    m_yC = [0] * SL; m_y = [0] * SL; m_comb = [0] * SL
    m_wuse = [0] * SL; m_xuse = [0] * NSEG; m_csts = [0]; m_aprep = [0]

    import contextlib as _ctl
    with _ctl.ExitStack() as _es:
        _sb = lambda n, sh, dt: _es.enter_context(nc.sbuf_tensor(n, sh, dt))
        _ps = lambda n, sh, dt: _es.enter_context(nc.psum_tensor(n, sh, dt))
        xnh = _sb("p2_xnh", [128, KC, HL], BF16)
        win0 = _sb("p2_win0", [128, KC, 128], BF16)
        win1 = _sb("p2_win1", [128, KC, 128], BF16)
        wgt0 = _sb("p2_wgt0", [128, KC, 128], BF16)
        wgt1 = _sb("p2_wgt1", [128, KC, 128], BF16)
        cwb = _sb("p2_cwb", [128, NG, 4], F32)
        cbb = _sb("p2_cbb", [128, NG], F32)
        Cb = _sb("p2_Cb", [128, NG], F32)
        Db = _sb("p2_Db", [128, NG], F32)
        Bb = _sb("p2_Bb", [128, NG], F32)
        Aneg = _sb("p2_Aneg", [128, NG], F32)
        Atmp = _sb("p2_Atmp", [128, NG], F32)
        alogp = _sb("p2_alogp", [128, NG], F32)
        bc2 = _sb("p2_bc2", [2, 128], F32)
        dtw16 = _sb("p2_dtw16", [128, NG, 2], BF16)
        dtb = _sb("p2_dtb", [2, NG], F32)
        carry = _sb("p2_carry", [128, NG], F32)
        tails = _sb("p2_tails", [128, NG, 3], F32)
        projp0 = _sb("p2_projp0", [128, HL + 3], F32)
        projp1 = _sb("p2_projp1", [128, HL + 3], F32)
        bA0 = _sb("p2_bA0", [128, HL], F32)
        bA1 = _sb("p2_bA1", [128, HL], F32)
        bB0 = _sb("p2_bB0", [128, HL], F32)
        bB1 = _sb("p2_bB1", [128, HL], F32)
        bC0 = _sb("p2_bC0", [128, HL], F32)
        bC1 = _sb("p2_bC1", [128, HL], F32)
        x160 = _sb("p2_x160", [128, HL], BF16)
        x161 = _sb("p2_x161", [128, HL], BF16)
        gs0 = _sb("p2_gs0", [128, HL], BF16)
        gs1 = _sb("p2_gs1", [128, HL], BF16)
        cmb0 = _sb("p2_cmb0", [128, HL], BF16)
        cmb1 = _sb("p2_cmb1", [128, HL], BF16)
        dtz0 = _sb("p2_dtz0", [2, HL], F32)
        dtz1 = _sb("p2_dtz1", [2, HL], F32)
        dte0 = _sb("p2_dte0", [2, HL], F32)
        dte1 = _sb("p2_dte1", [2, HL], F32)
        dtg0 = _sb("p2_dtg0", [2, HL], F32)
        dtg1 = _sb("p2_dtg1", [2, HL], F32)
        psA = _ps("p2_psA", [128, 2048], F32)
        psB = _ps("p2_psB", [128, 2048], F32)

        wins = [win0, win1]; wgts = [wgt0, wgt1]
        projps = [projp0, projp1]
        bAs = [bA0, bA1]; bBs = [bB0, bB1]; bCs = [bC0, bC1]
        x16s = [x160, x161]; gss = [gs0, gs1]; cmbs = [cmb0, cmb1]
        dtzs = [dtz0, dtz1]; dtes = [dte0, dte1]; dtgs = [dtg0, dtg1]

        def in_bank(n):
            return psA[:, 512 * (n % 2):512 * (n % 2) + 512]

        def gt_bank(n):
            return psA[:, 1024 + 512 * (n % 2):1024 + 512 * (n % 2) + 512]

        def bc_bank(n):
            return psB[:, 512 * (n % 2):512 * (n % 2) + 512]

        def dt_bank():
            return psB[0:2, 1024:1536]

        def prevn(s, n, k):
            idx = s * NCH + n - k
            return idx // NCH, idx % NCH

        def vbody(v):
            p.emit_consts(v)
            m_csts[0] = ve.do(lambda: v.memset(carry[:], 0.0))
            for S in range(NSEG):
                for g in range(NGR):
                    s = sidx(S, g)
                    par = s % 2
                    projp = projps[par]
                    bA, bB, bC = bAs[par], bBs[par], bCs[par]
                    if S == 0:
                        ve.do(lambda projp=projp: v.memset(projp[:, 0:3], 0.0))
                    else:
                        ve.do(lambda projp=projp, g=g: v.tensor_copy(
                            out=projp[:, 0:3], in_=tails[:, g, :]))
                    ve.w(se, m_pevac[s][NCH - 1])
                    if "conv" in P2OFF:
                        ve.do(lambda projp=projp, bA=bA: v.tensor_copy(
                            out=bA[:], in_=projp[:, 0:HL]))
                        ve.do(lambda projp=projp, bB=bB: v.tensor_copy(
                            out=bB[:], in_=projp[:, 1:HL + 1]))
                        ve.do(lambda projp=projp, bA=bA: v.tensor_copy(
                            out=bA[:], in_=projp[:, 2:HL + 2]))
                        m_conv[s] = ve.do(lambda projp=projp, bB=bB:
                                          v.tensor_copy(out=bB[:],
                                                        in_=projp[:, 3:HL + 3]))
                    else:
                        ve.do(lambda projp=projp, bA=bA, g=g: v.tensor_scalar(
                            out=bA[:], in0=projp[:, 0:HL], scalar1=cwb[:, g, 0:1],
                            scalar2=cbb[:, g:g + 1], op0=MULT, op1=ADD))
                        ve.do(lambda projp=projp, bA=bA, bB=bB, g=g:
                              v.scalar_tensor_tensor(
                                  out=bB[:], in0=projp[:, 1:HL + 1],
                                  scalar=cwb[:, g, 1:2], in1=bA[:],
                                  op0=MULT, op1=ADD))
                        ve.do(lambda projp=projp, bA=bA, bB=bB, g=g:
                              v.scalar_tensor_tensor(
                                  out=bA[:], in0=projp[:, 2:HL + 2],
                                  scalar=cwb[:, g, 2:3], in1=bB[:],
                                  op0=MULT, op1=ADD))
                        m_conv[s] = ve.do(lambda projp=projp, bA=bA, bB=bB, g=g:
                                          v.scalar_tensor_tensor(
                                              out=bB[:], in0=projp[:, 3:HL + 3],
                                              scalar=cwb[:, g, 3:4], in1=bA[:],
                                              op0=MULT, op1=ADD))
                    if S < NSEG - 1:
                        ve.do(lambda projp=projp, g=g: v.tensor_copy(
                            out=tails[:, g, :], in_=projp[:, HL:HL + 3]))
                    if s >= 2:
                        ve.w(se, m_dtexp[s - 2])
                    for n in range(NCH):
                        ve.w(te, m_mmdt[s][n])
                        if "dtev" in P2OFF:
                            m_dtevq[s][n] = ve.do(lambda par=par, n=n:
                                                  v.tensor_copy(
                                                      out=dtzs[par][0:2,
                                                                    512 * n:512 * n + 512],
                                                      in_=dt_bank()[0:2, 0:512]))
                        else:
                            m_dtevq[s][n] = ve.do(lambda par=par, n=n, g=g:
                                                  v.tensor_scalar(
                                                      out=dtzs[par][0:2,
                                                                    512 * n:512 * n + 512],
                                                      in0=dt_bank()[0:2, 0:512],
                                                      scalar1=dtb[0:2, g:g + 1],
                                                      scalar2=None, op0=ADD))
                    if s >= 2:
                        ve.w(te, m_mmbc[s - 2][NCH - 1])
                    ve.w(se, m_dtl[s])
                    if "dtadd" in P2OFF:
                        m_dtg[s] = ve.do(lambda par=par: v.tensor_copy(
                            out=dtes[par][:], in_=dtzs[par][:]))
                    else:
                        m_dtg[s] = ve.do(lambda par=par: v.tensor_tensor(
                            out=dtes[par][:], in0=dtzs[par][:], in1=dtgs[par][:],
                            op=ADD))
                    ve.w(se, m_X[s])
                    for n in range(NCH):
                        ve.w(te, m_mmbc[s][n])
                        ve.w(se, m_aevac[s][n])
                        if "uop" in P2OFF:
                            m_uevac[s][n] = ve.do(lambda bB=bB, n=n:
                                                  v.tensor_copy(
                                                      out=bB[:, 512 * n:512 * n + 512],
                                                      in_=bc_bank(n)))
                        else:
                            m_uevac[s][n] = ve.do(lambda bB=bB, bC=bC, n=n, g=g:
                                                  v.scalar_tensor_tensor(
                                                      out=bB[:, 512 * n:512 * n + 512],
                                                      in0=bc_bank(n),
                                                      scalar=Bb[:, g:g + 1],
                                                      in1=bC[:, 512 * n:512 * n + 512],
                                                      op0=MULT, op1=MULT))
                    ve.w(se, m_aevac[s][NCH - 1])
                    if "scan" in P2OFF:
                        m_scan[s] = ve.do(lambda projp=projp, bB=bB:
                                          v.tensor_copy(out=projp[:, 0:HL],
                                                        in_=bB[:]))
                    else:
                        m_scan[s] = ve.do(lambda projp=projp, bA=bA, bB=bB, g=g:
                                          v.tensor_tensor_scan(
                                              out=projp[:, 0:HL], data0=bA[:],
                                              data1=bB[:],
                                              initial=carry[:, g:g + 1],
                                              op0=MULT, op1=ADD))
                    ve.do(lambda projp=projp, g=g: v.tensor_copy(
                        out=carry[:, g:g + 1], in_=projp[:, HL - 1:HL]))
                    if "yop" in P2OFF:
                        m_yC[s] = ve.do(lambda projp=projp, bA=bA:
                                        v.tensor_copy(out=bA[:],
                                                      in_=projp[:, 0:HL]))
                        m_y[s] = ve.do(lambda bA=bA, bB=bB:
                                       v.tensor_copy(out=bB[:], in_=bA[:]))
                    else:
                        m_yC[s] = ve.do(lambda projp=projp, bA=bA, g=g:
                                        v.tensor_scalar(
                                            out=bA[:], in0=projp[:, 0:HL],
                                            scalar1=Cb[:, g:g + 1], scalar2=None,
                                            op0=MULT))
                        m_y[s] = ve.do(lambda bA=bA, bB=bB, bC=bC, g=g:
                                       v.scalar_tensor_tensor(
                                           out=bB[:], in0=bC[:],
                                           scalar=Db[:, g:g + 1], in1=bA[:],
                                           op0=MULT, op1=ADD))
                    ve.w(se, m_gevac[s][NCH - 1])
                    if s >= 2:
                        ve.w(stc[par], 16 * (s // 2))
                    m_comb[s] = ve.do(lambda bB=bB, par=par: v.tensor_tensor(
                        out=cmbs[par][:], in0=bB[:], in1=gss[par][:], op=MULT))

        def sbody(sc):
            se.w(cld, 16 * 8)
            se.w(cld16, 16)
            se.do(lambda: sc.activation(out=Atmp[:], in_=alogp[:], func=AF.Exp))
            m_aprep[0] = se.do(lambda: sc.mul(Aneg[:], Atmp[:], -1.0))
            for S in range(NSEG):
                for g in range(NGR):
                    s = sidx(S, g)
                    par = s % 2
                    projp = projps[par]
                    bB, bC = bBs[par], bCs[par]
                    for n in range(NCH):
                        se.w(te, m_mmin[s][n])
                        if n == 0 and s >= 2:
                            se.w(ve, m_yC[s - 2])
                        m_pevac[s][n] = se.do(lambda projp=projp, n=n: sc.copy(
                            out=projp[:, 3 + 512 * n:3 + 512 * n + 512],
                            in_=in_bank(n)))
                    if s >= 2:
                        se.w(ve, m_comb[s - 2])
                    for n in range(NCH):
                        se.w(te, m_mmgt[s][n])
                        m_gevac[s][n] = se.do(lambda par=par, n=n: sc.activation(
                            out=gss[par][:, 512 * n:512 * n + 512],
                            in_=gt_bank(n),
                            func=AF.Copy if "sgev" in P2OFF else AF.Sigmoid))
                    se.w(ve, m_conv[s])
                    if s >= 2:
                        se.w(ve, m_y[s - 2])
                        se.w(te, m_mmdt[s - 2][NCH - 1])
                    m_X[s] = se.do(lambda bB=bB, bC=bC: sc.activation(
                        out=bC[:], in_=bB[:],
                        func=AF.Copy if "silu" in P2OFF else AF.Silu))
                    m_X16[s] = se.do(lambda bC=bC, par=par: sc.copy(
                        out=x16s[par][:], in_=bC[:]))
                    se.w(ve, m_dtevq[s][NCH - 1])
                    if s >= 2:
                        se.w(te, m_mmbc[s - 2][NCH - 1])
                    m_dtexp[s] = se.do(lambda par=par: sc.activation(
                        out=dtes[par][:], in_=dtzs[par][:],
                        func=AF.Copy if "exl" in P2OFF else AF.Exp,
                        scale=-1.0))
                    if "exl" in P2OFF:
                        m_dtl[s] = se.do(lambda par=par: sc.copy(
                            out=dtgs[par][:], in_=dtes[par][:]))
                    else:
                        m_dtl[s] = se.do(lambda par=par: sc.activation(
                            out=dtgs[par][:], in_=dtes[par][:], func=AF.Ln,
                            bias=1.0))
                    for n in range(NCH):
                        se.w(te, m_mmbc[s][n])
                        if "aev" in P2OFF:
                            m_aevac[s][n] = se.do(lambda bA=bAs[par], n=n:
                                                  sc.copy(
                                                      out=bA[:, 512 * n:512 * n + 512],
                                                      in_=bc_bank(n)))
                        else:
                            m_aevac[s][n] = se.do(lambda bA=bAs[par], n=n, g=g:
                                                  sc.activation(
                                                      out=bA[:, 512 * n:512 * n + 512],
                                                      in_=bc_bank(n), func=AF.Exp,
                                                      scale=Aneg[:, g:g + 1]))

        def tbody(tn):
            te.w(ve, m_csts[0])
            te.w(se, m_aprep[0])
            te.w(cld, 16 * 8)

            def emit_ingate(S, g):
                if "noingate" in P2OFF:
                    m_wuse[sidx(S, g)] = te.n
                    return
                s = sidx(S, g)
                par = s % 2
                te.w(wld[par], 32 * (s // 2 + 1))
                te.w(xnl[S], 16 * KC)
                for n in range(NCH):
                    ps_, pn = prevn(s, n, 2)
                    if ps_ >= 0:
                        te.w(se, m_pevac[ps_][pn])
                    for k in range(KC):
                        cnt = te.do(lambda par=par, k=k, n=n: tn.matmul(
                            in_bank(n), wins[par][:, k, :],
                            xnh[:, k, 512 * n:512 * n + 512],
                            start=(k == 0), stop=(k == KC - 1)), selfsync=False)
                    m_mmin[s][n] = cnt
                for n in range(NCH):
                    ps_, pn = prevn(s, n, 2)
                    if ps_ >= 0:
                        te.w(se, m_gevac[ps_][pn])
                    for k in range(KC):
                        cnt = te.do(lambda par=par, k=k, n=n: tn.matmul(
                            gt_bank(n), wgts[par][:, k, :],
                            xnh[:, k, 512 * n:512 * n + 512],
                            start=(k == 0), stop=(k == KC - 1)), selfsync=False)
                    m_mmgt[s][n] = cnt
                m_wuse[s] = te.n

            def emit_small(S, g):
                if "nosmall" in P2OFF:
                    return
                s = sidx(S, g)
                par = s % 2
                te.w(se, m_X16[s])
                for n in range(NCH):
                    ps_, pn = prevn(s, n, 1)
                    if ps_ >= 0:
                        te.w(ve, m_dtevq[ps_][pn])
                    m_mmdt[s][n] = te.do(lambda g=g, par=par, n=n: tn.matmul(
                        dt_bank(), dtw16[:, g, :],
                        x16s[par][:, 512 * n:512 * n + 512],
                        start=True, stop=True), selfsync=False)
                ve_gate = m_dtg[s]
                te.w(ve, ve_gate)
                for n in range(NCH):
                    ps_, pn = prevn(s, n, 2)
                    if ps_ >= 0:
                        te.w(se, m_aevac[ps_][pn])
                        te.w(ve, m_uevac[ps_][pn])
                    m_mmbc[s][n] = te.do(lambda par=par, n=n: tn.matmul(
                        bc_bank(n), bc2[0:2, :],
                        dtes[par][0:2, 512 * n:512 * n + 512],
                        start=True, stop=True), selfsync=False)

            for S in range(NSEG):
                for g in range(NGR):
                    if g == 0 and S > 0:
                        emit_small(S - 1, NGR - 1)
                    emit_ingate(S, g)
                    if g >= 1:
                        emit_small(S, g - 1)
                m_xuse[S] = te.n
            emit_small(NSEG - 1, NGR - 1)

        def gbody(g_):
            p.dma(g_, dtw16[:], t["dtw_blk"][:], cld16)
            for S in range(NSEG):
                for g in range(NGR):
                    s = sidx(S, g)
                    par = s % 2
                    if s >= 2:
                        p.qwait(g_, te, m_wuse[s - 2])
                    p.dma(g_, wins[par][:],
                          t["w_in_T"][:, 128 * g:128 * (g + 1)]
                          .rearrange("(k pp) m -> pp k m", pp=128), wld[par])
                    p.dma(g_, wgts[par][:],
                          t["w_gate_T"][:, 128 * g:128 * (g + 1)]
                          .rearrange("(k pp) m -> pp k m", pp=128), wld[par])

        def syncbody(sync):
            p.dma(sync, cwb[:], t["conv_w_blk"][:], cld)
            p.dma(sync, cbb[:], t["conv_b_blk"][:], cld)
            p.dma(sync, Cb[:], t["C_blk"][:], cld)
            p.dma(sync, Db[:], t["D_blk"][:], cld)
            p.dma(sync, Bb[:], t["B_blk_flat"][:], cld)
            p.dma(sync, dtb[:], t["dt_b2"][:], cld)
            p.dma(sync, alogp[:], t["Alog_pp"][:], cld)
            p.dma(sync, bc2[:], t["bc2"][:], cld)
            for S in range(NSEG):
                if S >= 1:
                    p.qwait(sync, te, m_xuse[S - 1])
                for k in range(KC):
                    p.dma(sync, xnh[:, k, :],
                          t["xn_scr"][k, :, HL * S:HL * (S + 1)], xnl[S])
                for g in range(NGR):
                    s = sidx(S, g)
                    p.qwait(sync, ve, m_comb[s])
                    p.dma(sync, t["comb_scr"][g, :, HL * S:HL * (S + 1)],
                          cmbs[s % 2][:], stc[s % 2])
            for q in range(2):
                p.qwait(sync, stc[q], stc[q].n)

        bodies = {"vector": vbody, "scalar": sbody, "tensor": tbody,
                  "gpsimd": gbody, "sync": syncbody}
        if "dmaonly" in P2OFF:
            bodies = {"gpsimd": gbody, "sync": syncbody}
        for nm in ("vector", "scalar", "tensor"):
            if "no_" + nm in P2OFF and nm in bodies:
                del bodies[nm]
        p.two_pass(bodies)
    return



def emit_mm_ft(p, tag, wname, rhs_name, out_name, C, DO):
    nc, t = p.nc, p.t
    ve, se, te = p.ve, p.se, p.te
    CK = C // 128
    DK = DO // 128
    HLF = L // 2
    NCHF = HLF // 512
    rl = [p.sem(tag + "_rl%d" % h) for h in range(2)]
    wl = [p.sem(tag + "_wl0"), p.sem(tag + "_wl1")]
    so = [p.sem(tag + "_so0"), p.sem(tag + "_so1")]
    SLT = 2 * DK

    m_mm = [[0] * NCHF for _ in range(SLT)]
    m_ev = [[0] * NCHF for _ in range(SLT)]
    m_stg = [0] * SLT
    m_wu = [0] * SLT
    m_xu = [0] * 2

    import contextlib as _ctl
    with _ctl.ExitStack() as _es:
        _sb = lambda n, sh, dt: _es.enter_context(nc.sbuf_tensor(n, sh, dt))
        _ps = lambda n, sh, dt: _es.enter_context(nc.psum_tensor(n, sh, dt))
        rh = _sb(tag + "_rh", [128, CK, HLF], BF16)
        w0 = _sb(tag + "_w0", [128, CK, 128], BF16)
        w1 = _sb(tag + "_w1", [128, CK, 128], BF16)
        st0 = _sb(tag + "_st0", [128, HLF], BF16)
        st1 = _sb(tag + "_st1", [128, HLF], BF16)
        psO = _ps(tag + "_ps", [128, 1024], F32)
        ws = [w0, w1]
        sts = [st0, st1]

        def sidx(H, d):
            return H * DK + d

        def tbody(tn):
            for H in range(2):
                te.w(rl[H], 16 * CK)
                for d in range(DK):
                    s = sidx(H, d)
                    par = s % 2
                    te.w(wl[par], 16 * (s // 2 + 1))
                    for n in range(NCHF):
                        idx = s * NCHF + n
                        if idx >= 2:
                            pi = idx - 2
                            te.w(se, m_ev[pi // NCHF][pi % NCHF])
                        for k in range(CK):
                            cnt = te.do(lambda par=par, k=k, n=n: tn.matmul(
                                psO[:, 512 * (n % 2):512 * (n % 2) + 512],
                                ws[par][:, k, :],
                                rh[:, k, 512 * n:512 * n + 512],
                                start=(k == 0), stop=(k == CK - 1)),
                                selfsync=False)
                        m_mm[s][n] = cnt
                    m_wu[s] = te.n
                m_xu[H] = te.n

        def sbody(sc):
            for H in range(2):
                for d in range(DK):
                    s = sidx(H, d)
                    par = s % 2
                    for n in range(NCHF):
                        se.w(te, m_mm[s][n])
                        if s >= 2 and n == 0:
                            se.w(so[par], 16 * (s // 2))
                        m_ev[s][n] = se.do(lambda par=par, n=n: sc.copy(
                            out=sts[par][:, 512 * n:512 * n + 512],
                            in_=psO[:, 512 * (n % 2):512 * (n % 2) + 512]))
                    m_stg[s] = m_ev[s][NCHF - 1]

        def gbody(g_):
            for H in range(2):
                for d in range(DK):
                    s = sidx(H, d)
                    par = s % 2
                    if s >= 2:
                        p.qwait(g_, te, m_wu[s - 2])
                    p.dma(g_, ws[par][:],
                          t[wname][:, 128 * d:128 * (d + 1)]
                          .rearrange("(k pp) m -> pp k m", pp=128), wl[par])

        def syncbody(sync):
            for H in range(2):
                if H == 1:
                    p.qwait(sync, te, m_xu[0])
                for k in range(CK):
                    p.dma(sync, rh[:, k, :],
                          t[rhs_name][k, :, HLF * H:HLF * (H + 1)], rl[H])
                for d in range(DK):
                    s = sidx(H, d)
                    par = s % 2
                    p.qwait(sync, se, m_stg[s])
                    p.dma(sync, t[out_name][d, :, HLF * H:HLF * (H + 1)],
                          sts[par][:], so[par])
            for q in range(2):
                p.qwait(sync, so[q], so[q].n)

        p.two_pass({"tensor": tbody, "scalar": sbody, "gpsimd": gbody,
                    "sync": syncbody})
    return


def emit_p3(p):
    emit_mm_ft(p, "p3", "w_out_T", "comb_scr", "attn0_scr", INNER, D)



def emit_p5(p):
    nc, t = p.nc, p.t
    ve, se, te = p.ve, p.se, p.te
    HLF = L // 2
    il = [p.sem("p5_il0"), p.sem("p5_il1")]
    x1l = [p.sem("p5_x1l0"), p.sem("p5_x1l1")]
    sx = [p.sem("p5_sx0"), p.sem("p5_sx1")]
    sn = [p.sem("p5_sn0"), p.sem("p5_sn1")]
    m_x1 = [[0] * KC for _ in range(2)]
    m_sq = [[0] * KC for _ in range(2)]
    m_summ = [[0] * KC for _ in range(2)]
    m_sqmm = [[0] * KC for _ in range(2)]
    m_stev = [0] * 2
    m_ab = [[0] * 4 for _ in range(2)]
    m_abev = [[0] * 4 for _ in range(2)]
    m_vareps = [0] * 2
    m_bb = [0] * 2
    m_xn = [[0] * KC for _ in range(2)]
    m_ones = [0]

    import contextlib as _ctl
    with _ctl.ExitStack() as _es:
        _sb = lambda n, sh, dt: _es.enter_context(nc.sbuf_tensor(n, sh, dt))
        _ps = lambda n, sh, dt: _es.enter_context(nc.psum_tensor(n, sh, dt))
        xt0 = _sb("p5_xt0", [128, HLF], F32)
        xt1 = _sb("p5_xt1", [128, HLF], F32)
        at0 = _sb("p5_at0", [128, HLF], BF16)
        at1 = _sb("p5_at1", [128, HLF], BF16)
        ft0 = _sb("p5_ft0", [128, HLF], BF16)
        ft1 = _sb("p5_ft1", [128, HLF], BF16)
        tb0 = _sb("p5_tb0", [128, HLF], F32)
        tb1 = _sb("p5_tb1", [128, HLF], F32)
        x10 = _sb("p5_x10", [128, HLF], F32)
        x11 = _sb("p5_x11", [128, HLF], F32)
        sq = _sb("p5_sq", [128, HLF], F32)
        ones32 = _sb("p5_ones", [128, 1], F32)
        ones_row = _sb("p5_onesr", [1, 128], F32)
        srow = _sb("p5_srow", [1, HLF], F32)
        qrow = _sb("p5_qrow", [1, HLF], F32)
        stt_ = _sb("p5_stt", [1, 8 * HLF], F32)
        ab_b = _sb("p5_abb", [128, HLF], F32)
        bb_b = _sb("p5_bbb", [128, HLF], F32)
        xnt0 = _sb("p5_xnt0", [128, HLF], BF16)
        xnt1 = _sb("p5_xnt1", [128, HLF], BF16)
        psA = _ps("p5_psA", [128, 2048], F32)
        psB = _ps("p5_psB", [128, 2048], F32)
        xts = [xt0, xt1]; ats = [at0, at1]; fts = [ft0, ft1]
        tbs = [tb0, tb1]; x1s = [x10, x11]; xnts = [xnt0, xnt1]
        NCHF = HLF // 512

        def row(i):
            return stt_[0:1, HLF * i:HLF * (i + 1)]

        def vbody(v):
            p.emit_consts(v)
            ve.do(lambda: v.memset(ones32[:], 1.0))
            m_ones[0] = ve.do(lambda: v.memset(ones_row[:], 1.0))
            for H in range(2):
                for d in range(KC):
                    par = d % 2
                    ve.w(il[par], 16 * 3 * ((H * KC + d) // 2 + 1))
                    if H * KC + d >= 2:
                        pH, pd = (H, d - 2) if d >= 2 else (H - 1, KC - 2 + d)
                        if pH >= 0:
                            ve.w(se, m_sq[pH][pd])
                            ve.w(te, m_summ[pH][pd])
                    ve.do(lambda par=par: v.tensor_tensor(
                        out=tbs[par][:], in0=xts[par][:], in1=ats[par][:], op=ADD))
                    m_x1[H][d] = ve.do(lambda par=par: v.tensor_tensor(
                        out=x1s[par][:], in0=tbs[par][:], in1=fts[par][:], op=ADD))
                ve.w(se, m_stev[H])
                ve.do(lambda: v.tensor_scalar(out=row(0), in0=srow[:],
                                              scalar1=-1.0 / D, scalar2=None,
                                              op0=MULT))
                ve.do(lambda: v.tensor_scalar(out=row(1), in0=qrow[:],
                                              scalar1=1.0 / D, scalar2=None,
                                              op0=MULT))
                ve.do(lambda: v.tensor_tensor(out=row(2), in0=row(0), in1=row(0),
                                              op=MULT))
                ve.do(lambda: v.tensor_tensor(out=row(3), in0=row(1), in1=row(2),
                                              op=SUB))
                m_vareps[H] = ve.do(lambda: v.tensor_scalar(
                    out=row(4), in0=row(3), scalar1=EPS, scalar2=None, op0=ADD))
                ve.w(se, m_stev[H] + 1)
                ve.do(lambda: v.reciprocal(out=row(6), in_=row(5)))
                m_bb[H] = ve.do(lambda: v.tensor_tensor(
                    out=row(7), in0=row(0), in1=row(6), op=MULT))
                for d in range(KC):
                    par = d % 2
                    ve.w(x1l[par], 16 * ((H * KC + d) // 2 + 1))
                    ve.w(se, m_abev[H][NCHF - 1])
                    if H * KC + d >= 2:
                        pidx = H * KC + d - 2
                        ve.w(sn[par], 16 * (pidx // 2 + 1))
                    ve.do(lambda par=par: v.tensor_tensor(
                        out=tbs[par][:], in0=x1s[par][:], in1=ab_b[:], op=MULT))
                    m_xn[H][d] = ve.do(lambda par=par: v.tensor_tensor(
                        out=xnts[par][:], in0=tbs[par][:], in1=bb_b[:], op=ADD))

        def sbody(sc):
            for H in range(2):
                for d in range(KC):
                    par = d % 2
                    se.w(ve, m_x1[H][d])
                    if H * KC + d >= 1:
                        pv = m_sqmm[H][d - 1] if d >= 1 else m_sqmm[H - 1][KC - 1]
                        se.w(te, pv)
                    m_sq[H][d] = se.do(lambda par=par: sc.activation(
                        out=sq[:], in_=x1s[par][:], func=AF.Square))
                se.w(te, m_sqmm[H][KC - 1])
                se.do(lambda: sc.copy(out=srow[:], in_=psA[0:1, 0:HLF]))
                m_stev[H] = se.do(lambda: sc.copy(out=qrow[:], in_=psB[0:1, 0:HLF]))
                se.w(ve, m_vareps[H])
                se.do(lambda: sc.activation(out=row(5), in_=row(4), func=AF.Sqrt))
                if H >= 1:
                    se.w(ve, m_xn[H - 1][KC - 1])
                for n in range(NCHF):
                    se.w(te, m_ab[H][n])
                    m_abev[H][n] = se.do(lambda n=n: sc.copy(
                        out=ab_b[:, 512 * n:512 * n + 512],
                        in_=psA[:, 512 * n:512 * n + 512]))
                    m_abev[H][n] = se.do(lambda n=n: sc.copy(
                        out=bb_b[:, 512 * n:512 * n + 512],
                        in_=psB[:, 512 * n:512 * n + 512]))

        def tbody(tn):
            te.w(ve, m_ones[0])
            for H in range(2):
                if H >= 1:
                    te.w(se, m_abev[H - 1][NCHF - 1])
                for d in range(KC):
                    par = d % 2
                    te.w(ve, m_x1[H][d])
                    for n in range(NCHF):
                        cnt = te.do(lambda par=par, n=n, d=d: tn.matmul(
                            psA[0:1, 512 * n:512 * n + 512], ones32[:],
                            x1s[par][:, 512 * n:512 * n + 512],
                            start=(d == 0), stop=(d == KC - 1)), selfsync=False)
                    m_summ[H][d] = cnt
                    te.w(se, m_sq[H][d])
                    for n in range(NCHF):
                        cnt = te.do(lambda n=n, d=d: tn.matmul(
                            psB[0:1, 512 * n:512 * n + 512], ones32[:],
                            sq[:, 512 * n:512 * n + 512],
                            start=(d == 0), stop=(d == KC - 1)), selfsync=False)
                    m_sqmm[H][d] = cnt
                te.w(se, m_stev[H])
                te.w(ve, m_bb[H])
                for n in range(NCHF):
                    m_ab[H][n] = te.do(lambda n=n: tn.matmul(
                        psA[:, 512 * n:512 * n + 512], ones_row[:],
                        row(6)[0:1, 512 * n:512 * n + 512],
                        start=True, stop=True), selfsync=False)
                    m_ab[H][n] = te.do(lambda n=n: tn.matmul(
                        psB[:, 512 * n:512 * n + 512], ones_row[:],
                        row(7)[0:1, 512 * n:512 * n + 512],
                        start=True, stop=True), selfsync=False)

        def syncbody(sync):
            for H in range(2):
                for d in range(KC):
                    par = d % 2
                    idx = H * KC + d
                    if idx >= 2:
                        p.qwait(sync, ve, m_x1[H][d - 2] if d >= 2
                                else m_x1[H - 1][KC - 2 + d])
                        p.qwait(sync, se, m_sq[H][d - 2] if d >= 2
                                else m_sq[H - 1][KC - 2 + d])
                        p.qwait(sync, te, m_summ[H][d - 2] if d >= 2
                                else m_summ[H - 1][KC - 2 + d])
                    p.dma(sync, xts[par][:],
                          t["xT_scr"][d, :, HLF * H:HLF * (H + 1)], il[par])
                    p.dma(sync, ats[par][:],
                          t["attn0_scr"][d, :, HLF * H:HLF * (H + 1)], il[par])
                    p.dma(sync, fts[par][:],
                          t["ffn0_scr"][d, :, HLF * H:HLF * (H + 1)], il[par])
                    p.qwait(sync, ve, m_x1[H][d])
                    p.dma(sync, t["x1_scr"][d, :, HLF * H:HLF * (H + 1)],
                          x1s[par][:], sx[par])
                p.qwait(sync, se, m_sq[H][KC - 2])
                p.qwait(sync, se, m_sq[H][KC - 1])
                p.qwait(sync, te, m_summ[H][KC - 1])
                for d in range(KC):
                    par = d % 2
                    idx = H * KC + d
                    if idx >= 2:
                        p.qwait(sync, ve, m_xn[H][d - 2] if d >= 2
                                else m_xn[H - 1][KC - 2 + d])
                    p.qwait(sync, sx[par], 16 * (idx // 2 + 1))
                    p.dma(sync, x1s[par][:],
                          t["x1_scr"][d, :, HLF * H:HLF * (H + 1)], x1l[par])
                    p.qwait(sync, ve, m_xn[H][d])
                    p.dma(sync, t["xn1_scr"][d, :, HLF * H:HLF * (H + 1)],
                          xnts[par][:], sn[par])
            for q in range(2):
                p.qwait(sync, sn[q], sn[q].n)
                p.qwait(sync, sx[q], sx[q].n)

        p.two_pass({"vector": vbody, "scalar": sbody, "tensor": tbody,
                    "sync": syncbody})
    return


def emit_p10(p):
    nc, t = p.nc, p.t
    ve, se, te = p.ve, p.se, p.te
    HLF = L // 2
    il = [p.sem("p10_il0"), p.sem("p10_il1")]
    so = [p.sem("p10_so0"), p.sem("p10_so1")]
    idld = p.sem("p10_idld")
    m_f = [[0] * KC for _ in range(2)]
    m_tr = [[[0] * 4 for _ in range(16)] for _ in range(2)]
    m_ev = [[[0] * 4 for _ in range(16)] for _ in range(2)]
    m_yst = [[0] * 16 for _ in range(2)]

    import contextlib as _ctl
    with _ctl.ExitStack() as _es:
        _sb = lambda n, sh, dt: _es.enter_context(nc.sbuf_tensor(n, sh, dt))
        _ps = lambda n, sh, dt: _es.enter_context(nc.psum_tensor(n, sh, dt))
        fb = [_sb("p10_fb%d" % d, [128, HLF], F32) for d in range(KC)]
        a0 = _sb("p10_a0", [128, HLF], BF16)
        a1 = _sb("p10_a1", [128, HLF], BF16)
        f0 = _sb("p10_f0", [128, HLF], BF16)
        f1 = _sb("p10_f1", [128, HLF], BF16)
        x0 = _sb("p10_x0", [128, HLF], F32)
        x1_ = _sb("p10_x1", [128, HLF], F32)
        yst0 = _sb("p10_y0", [128, D], F32)
        yst1 = _sb("p10_y1", [128, D], F32)
        id32 = _sb("p10_id32", [128, 128], F32)
        tr32 = _ps("p10_tr", [128, 512], F32)
        aa = [a0, a1]; ff = [f0, f1]; xx = [x0, x1_]
        ys = [yst0, yst1]

        def vbody(v):
            for H in range(2):
                for d in range(KC):
                    par = d % 2
                    ve.w(il[par], 16 * 3 * ((H * KC + d) // 2 + 1))
                    if H == 1:
                        ve.w(te, m_tr[0][15][3])
                    ve.do(lambda par=par, d=d: v.tensor_tensor(
                        out=xx[par][:], in0=xx[par][:] if False else xx[par][:],
                        in1=aa[par][:], op=ADD)) if False else None
                    ve.do(lambda par=par, d=d: v.tensor_tensor(
                        out=fb[d][:], in0=xx[par][:], in1=aa[par][:], op=ADD))
                    m_f[H][d] = ve.do(lambda par=par, d=d: v.tensor_tensor(
                        out=fb[d][:] if False else xx[par][:], in0=fb[d][:],
                        in1=ff[par][:], op=ADD)) if False else 0
                    ve.do(lambda par=par, d=d: v.tensor_tensor(
                        out=xx[par][:], in0=fb[d][:], in1=ff[par][:], op=ADD))
                    m_f[H][d] = ve.do(lambda par=par, d=d: v.tensor_copy(
                        out=fb[d][:], in_=xx[par][:]))
                for lt in range(16):
                    gl = H * 16 + lt
                    for c4 in range(4):
                        ve.w(te, m_tr[H][lt][c4])
                        if c4 == 0 and gl >= 2:
                            ve.w(so[lt % 2], 16 * (gl // 2))
                        m_ev[H][lt][c4] = ve.do(lambda lt=lt, c4=c4: v.tensor_copy(
                            out=ys[lt % 2][:, 512 * c4:512 * (c4 + 1)],
                            in_=tr32[:]))

        def tbody(tn):
            te.w(idld, 16)
            for H in range(2):
                for lt in range(16):
                    for c4 in range(4):
                        for j in range(4):
                            d = 4 * c4 + j
                            te.w(ve, m_f[H][d])
                            if H + lt + c4 > 0:
                                prev = (m_ev[H][lt][c4 - 1] if c4 > 0 else
                                        (m_ev[H][lt - 1][3] if lt > 0 else
                                         m_ev[H - 1][15][3]))
                                te.w(ve, prev)
                            cnt = te.do(lambda H=H, lt=lt, d=d, j=j: tn.transpose(
                                tr32[:, 128 * j:128 * (j + 1)],
                                fb[d][:, 128 * lt:128 * (lt + 1)],
                                id32[:]), selfsync=False)
                        m_tr[H][lt][c4] = cnt

        def syncbody(sync):
            p.dma(sync, id32[:], t["ident"][:], idld)
            for H in range(2):
                for d in range(KC):
                    par = d % 2
                    idx = H * KC + d
                    if idx >= 2:
                        pv = m_f[H][d - 2] if d >= 2 else m_f[H - 1][KC - 2 + d]
                        p.qwait(sync, ve, pv)
                    p.dma(sync, xx[par][:],
                          t["x1_scr"][d, :, HLF * H:HLF * (H + 1)], il[par])
                    p.dma(sync, aa[par][:],
                          t["o1_scr"][d, :, HLF * H:HLF * (H + 1)], il[par])
                    p.dma(sync, ff[par][:],
                          t["ffn1_scr"][d, :, HLF * H:HLF * (H + 1)], il[par])
                for lt in range(16):
                    gl = H * 16 + lt
                    p.qwait(sync, ve, m_ev[H][lt][3])
                    p.dma(sync, t["y_out"][128 * gl:128 * (gl + 1), :],
                          ys[lt % 2][:], so[lt % 2])
            for q in range(2):
                p.qwait(sync, so[q], so[q].n)

        p.two_pass({"vector": vbody, "scalar": lambda sc: None,
                    "tensor": tbody, "sync": syncbody})
    return



def emit_p6(p):
    nc, t = p.nc, p.t
    ve, se, te = p.ve, p.se, p.te
    HLF = L // 2
    NCHF = HLF // 512
    cl = p.sem("p6_cl")
    rl = [p.sem("p6_rl%d" % h) for h in range(2)]
    wl = [p.sem("p6_wl0"), p.sem("p6_wl1")]
    sq_ = [p.sem("p6_sq0"), p.sem("p6_sq1")]
    vrl = [p.sem("p6_vrl%d" % h) for h in range(2)]
    wvl = p.sem("p6_wvl")
    sv = [p.sem("p6_sv0"), p.sem("p6_sv1")]
    DK = KC
    SLQ = 2 * 2 * DK

    def qidx(H, w, d):
        return (H * 2 + w) * DK + d

    m_mm = [[0] * NCHF for _ in range(SLQ)]
    m_ev = [[0] * NCHF for _ in range(SLQ)]
    m_sw = [[0] * NCHF for _ in range(SLQ)]
    m_rot = [[0] * NCHF for _ in range(SLQ)]
    m_wu = [0] * SLQ
    m_xuq = [0] * 2
    m_cst = [0]
    m_vmm = [[0] * NCHF for _ in range(NL)]
    m_vev = [[0] * NCHF for _ in range(NL)]
    m_xuv = [0] * 2

    import contextlib as _ctl
    with _ctl.ExitStack() as _es:
        _sb = lambda n, sh, dt: _es.enter_context(nc.sbuf_tensor(n, sh, dt))
        _ps = lambda n, sh, dt: _es.enter_context(nc.psum_tensor(n, sh, dt))
        xnh = _sb("p6_xnh", [128, KC, HLF], BF16)
        wv = _sb("p6_wv", [128, KC, D], BF16)
        w0 = _sb("p6_w0", [128, KC, 128], BF16)
        w1 = _sb("p6_w1", [128, KC, 128], BF16)
        cosb = _sb("p6_cos", [128, HLF], F32)
        sinb = _sb("p6_sin", [128, HLF], F32)
        psw = _sb("p6_psw", [128, 128], BF16)
        qs0 = _sb("p6_qs0", [128, 512], BF16)
        qs1 = _sb("p6_qs1", [128, 512], BF16)
        tmp0 = _sb("p6_tmp0", [128, 512], F32)
        tmp1 = _sb("p6_tmp1", [128, 512], F32)
        tmp2 = _sb("p6_tmp2", [128, 512], F32)
        tmp3 = _sb("p6_tmp3", [128, 512], F32)
        stg0 = _sb("p6_stg0", [128, HLF], BF16)
        stg1 = _sb("p6_stg1", [128, HLF], BF16)
        psQ = _ps("p6_psQ", [128, 1024], F32)
        psW = _ps("p6_psW", [128, 1024], F32)
        ws = [w0, w1]
        qss = [qs0, qs1]
        tmpa = [tmp0, tmp1]
        tmpb = [tmp2, tmp3]
        stgs = [stg0, stg1]

        wnames = ["Wq_T", "Wk_T"]
        scrq = ["q_scr", "k_scr"]

        def tbody(tn):
            te.w(cl, 16 * 3)
            for H in range(2):
                te.w(rl[H], 16 * KC)
                for w in range(2):
                    for d in range(DK):
                        s = qidx(H, w, d)
                        par = s % 2
                        te.w(wl[par], 16 * (s // 2 + 1))
                        for n in range(NCHF):
                            idx = s * NCHF + n
                            if idx >= 2:
                                pi = idx - 2
                                te.w(se, m_ev[pi // NCHF][pi % NCHF])
                            for k in range(KC):
                                cnt = te.do(lambda par=par, k=k, n=n: tn.matmul(
                                    psQ[:, 512 * (n % 2):512 * (n % 2) + 512],
                                    ws[par][:, k, :],
                                    xnh[:, k, 512 * n:512 * n + 512],
                                    start=(k == 0), stop=(k == KC - 1)),
                                    selfsync=False)
                            m_mm[s][n] = cnt
                            te.w(se, m_ev_pred(s, n))
                            if idx >= 2:
                                pi = idx - 2
                                te.w(ve, m_rot[pi // NCHF][pi % NCHF])
                            m_sw[s][n] = te.do(lambda n=n, par2=(s * NCHF + n) % 2:
                                               tn.matmul(
                                                   psW[:, 512 * ((n) % 2):
                                                       512 * ((n) % 2) + 512],
                                                   psw[:],
                                                   qss[(s * NCHF + n) % 2][:],
                                                   start=True, stop=True),
                                               selfsync=False)
                        m_wu[s] = te.n
                m_xuq[H] = te.n
            te.w(wvl, 16)
            te.w(se, m_ev[SLQ - 1][NCHF - 1])
            for H in range(2):
                te.w(vrl[H], 16 * KC)
                for lt in range(16):
                    gl = H * 16 + lt
                    for n in range(NCHF):
                        idx = gl * NCHF + n
                        if idx >= 2:
                            pi = idx - 2
                            te.w(se, m_vev[pi // NCHF][pi % NCHF])
                        for k in range(KC):
                            cnt = te.do(lambda lt=lt, k=k, n=n: tn.matmul(
                                psQ[:, 512 * (n % 2):512 * (n % 2) + 512],
                                xnh[:, k, 128 * lt:128 * (lt + 1)],
                                wv[:, k, 512 * n:512 * n + 512],
                                start=(k == 0), stop=(k == KC - 1)),
                                selfsync=False)
                        m_vmm[gl][n] = cnt
                m_xuv[H] = te.n

        def m_ev_pred(s, n):
            return m_ev[s][n]

        def sbody(sc):
            for H in range(2):
                for w in range(2):
                    for d in range(DK):
                        s = qidx(H, w, d)
                        for n in range(NCHF):
                            idx = s * NCHF + n
                            se.w(te, m_mm[s][n])
                            if idx >= 2:
                                pi = idx - 2
                                se.w(te, m_sw[pi // NCHF][pi % NCHF])
                                se.w(ve, m_rot[pi // NCHF][pi % NCHF])
                            m_ev[s][n] = se.do(lambda idx=idx, n=n: sc.copy(
                                out=qss[idx % 2][:],
                                in_=psQ[:, 512 * (n % 2):512 * (n % 2) + 512]))
            se.w(sq_[0], 16 * 32)
            se.w(sq_[1], 16 * 32)
            for H in range(2):
                for lt in range(16):
                    gl = H * 16 + lt
                    for n in range(NCHF):
                        se.w(te, m_vmm[gl][n])
                        if gl >= 2 and n == 0:
                            se.w(sv[gl % 2], 16 * (gl // 2))
                        m_vev[gl][n] = se.do(lambda gl=gl, n=n: sc.copy(
                            out=stgs[gl % 2][:, 512 * n:512 * n + 512],
                            in_=psQ[:, 512 * (n % 2):512 * (n % 2) + 512]))

        def vbody(v):
            p.emit_consts(v)
            m_cst[0] = ve.n
            for H in range(2):
                if H == 1:
                    ve.w(cl, 16 * 5)
                for w in range(2):
                    for d in range(DK):
                        s = qidx(H, w, d)
                        par = s % 2
                        for n in range(NCHF):
                            idx = s * NCHF + n
                            ve.w(se, m_ev[s][n])
                            ve.do(lambda idx=idx, n=n: v.tensor_tensor(
                                out=tmpa[idx % 2][:], in0=qss[idx % 2][:],
                                in1=cosb[:, 512 * n:512 * n + 512], op=MULT))
                            ve.w(te, m_sw[s][n])
                            ve.do(lambda idx=idx, n=n: v.tensor_tensor(
                                out=tmpb[idx % 2][:],
                                in0=psW[:, 512 * (n % 2):512 * (n % 2) + 512],
                                in1=sinb[:, 512 * n:512 * n + 512], op=MULT))
                            if n == 0 and s >= 2:
                                ve.w(sq_[par], 16 * (s // 2))
                            m_rot[s][n] = ve.do(lambda par=par, idx=idx, n=n:
                                                v.tensor_tensor(
                                                    out=stgs[par][:,
                                                                  512 * n:512 * n + 512],
                                                    in0=tmpa[idx % 2][:],
                                                    in1=tmpb[idx % 2][:], op=ADD))

        def gbody(g_):
            p.dma(g_, psw[:], t["pswap"][:], cl)
            for H in range(2):
                for w in range(2):
                    for d in range(DK):
                        s = qidx(H, w, d)
                        par = s % 2
                        if s >= 2:
                            p.qwait(g_, te, m_wu[s - 2])
                        p.dma(g_, ws[par][:],
                              t[wnames[w]][:, 128 * d:128 * (d + 1)]
                              .rearrange("(k pp) m -> pp k m", pp=128), wl[par])
            p.dma(g_, wv[:], t["Wv_T"][:].rearrange("(k pp) m -> pp k m", pp=128),
                  wvl)

        def syncbody(sync):
            p.dma(sync, cosb[:], t["rope_cos"][:, 0:HLF], cl)
            p.dma(sync, sinb[:], t["rope_sin"][:, 0:HLF], cl)
            for H in range(2):
                if H == 1:
                    p.qwait(sync, te, m_xuq[0])
                    p.qwait(sync, ve, m_rot[qidx(0, 1, DK - 1)][NCHF - 1])
                    p.dma(sync, cosb[:], t["rope_cos"][:, HLF:L], cl)
                    p.dma(sync, sinb[:], t["rope_sin"][:, HLF:L], cl)
                for k in range(KC):
                    p.dma(sync, xnh[:, k, :],
                          t["xn1_scr"][k, :, HLF * H:HLF * (H + 1)], rl[H])
                for w in range(2):
                    for d in range(DK):
                        s = qidx(H, w, d)
                        par = s % 2
                        p.qwait(sync, ve, m_rot[s][NCHF - 1])
                        p.dma(sync, t[scrq[w]][d, :, HLF * H:HLF * (H + 1)],
                              stgs[par][:], sq_[par])
            for H in range(2):
                if H == 0:
                    p.qwait(sync, te, m_xuq[1])
                if H == 1:
                    p.qwait(sync, te, m_xuv[0])
                for k in range(KC):
                    p.dma(sync, xnh[:, k, :],
                          t["xn1_scr"][k, :, HLF * H:HLF * (H + 1)], vrl[H])
                for lt in range(16):
                    gl = H * 16 + lt
                    p.qwait(sync, se, m_vev[gl][NCHF - 1])
                    p.dma(sync, t["v_scr"][gl].rearrange("pp h hd -> pp (h hd)"),
                          stgs[gl % 2][:], sv[gl % 2])
            for q in range(2):
                p.qwait(sync, sv[q], sv[q].n)
                p.qwait(sync, sq_[q], sq_[q].n)

        p.two_pass({"vector": vbody, "scalar": sbody, "tensor": tbody,
                    "gpsimd": gbody, "sync": syncbody})
    return



def emit_p7(p):
    nc, t = p.nc, p.t
    ve, se, te = p.ve, p.se, p.te
    NH = 16
    NQB = 8
    cl = p.sem("p7_cl")
    hl = [p.sem("p7_hl0"), p.sem("p7_hl1")]
    sa = [p.sem("p7_sa0"), p.sem("p7_sa1")]
    SC = 1.0 / float(np.sqrt(128.0))

    m_q2 = [0] * NH; m_k2 = [0] * NH
    m_nq = [[0] * 8 for _ in range(NH)]
    m_nk = [[0] * 8 for _ in range(NH)]
    m_nqe = [[0] * 8 for _ in range(NH)]
    m_nke = [[0] * 8 for _ in range(NH)]
    m_bnd = [0] * NH
    m_bndv = [0] * NH
    m_enbmm = [[0] * NQB for _ in range(NH)]
    m_enbev = [[0] * NQB for _ in range(NH)]
    m_masks = [[0] * NQB for _ in range(NH)]
    m_st = [[[0] * 36 for _ in range(NQB)] for _ in range(NH)]
    m_exp = [[[0] * 36 for _ in range(NQB)] for _ in range(NH)]
    m_pt = [[[0] * 36 for _ in range(NQB)] for _ in range(NH)]
    m_sum = [[0] * NQB for _ in range(NH)]
    m_av = [[0] * NQB for _ in range(NH)]
    m_sev = [[0] * NQB for _ in range(NH)]
    m_rec = [[0] * NQB for _ in range(NH)]
    m_rmm = [[0] * NQB for _ in range(NH)]
    m_rev = [[0] * NQB for _ in range(NH)]
    m_avev = [[0] * NQB for _ in range(NH)]
    m_hu = [0] * NH
    m_cst = [0]

    import contextlib as _ctl
    with _ctl.ExitStack() as _es:
        _sb = lambda n, sh, dt: _es.enter_context(nc.sbuf_tensor(n, sh, dt))
        _ps = lambda n, sh, dt: _es.enter_context(nc.psum_tensor(n, sh, dt))
        qh0 = _sb("p7_qh0", [128, L], BF16)
        qh1 = _sb("p7_qh1", [128, L], BF16)
        kh0 = _sb("p7_kh0", [128, L], BF16)
        kh1 = _sb("p7_kh1", [128, L], BF16)
        vb0 = _sb("p7_vb0", [128, NL, 128], BF16)
        vb1 = _sb("p7_vb1", [128, NL, 128], BF16)
        q2 = _sb("p7_q2", [128, L], F32)
        masks = _sb("p7_masks", [128, 4, 512], F32)
        onescol = _sb("p7_ones", [128, 1], BF16)
        onescol32 = _sb("p7_ones32", [128, 1], F32)
        ones1r = _sb("p7_ones1r", [1, 128], F32)
        nrow = _sb("p7_nrow", [1, 4 * L], F32)
        small = _sb("p7_small", [1, 8], F32)
        enbrow = _sb("p7_enbrow", [1, L], F32)
        enbb0 = _sb("p7_enbb0", [128, 512], F32)
        enbb1 = _sb("p7_enbb1", [128, 512], F32)
        enbm = _sb("p7_enbm", [128, 4, 512], F32)
        ebuf0 = _sb("p7_eb0", [128, 512], F32)
        ebuf1 = _sb("p7_eb1", [128, 512], F32)
        pbuf0 = _sb("p7_pb0", [128, 512], BF16)
        pbuf1 = _sb("p7_pb1", [128, 512], BF16)
        srow = _sb("p7_srow", [1, 512], F32)
        rrow = _sb("p7_rrow", [1, 512], F32)
        rbb0 = _sb("p7_rbb0", [128, 512], F32)
        rbb1 = _sb("p7_rbb1", [128, 512], F32)
        stg0 = _sb("p7_stg0", [128, L], BF16)
        stg1 = _sb("p7_stg1", [128, L], BF16)
        psST = _ps("p7_psST", [128, 1024], F32)
        psAV = _ps("p7_psAV", [128, 1024], F32)
        psSum = _ps("p7_psSum", [128, 1024], F32)
        psMsc = _ps("p7_psMsc", [128, 1024], F32)
        qhs = [qh0, qh1]; khs = [kh0, kh1]; vbs = [vb0, vb1]
        enbbs = [enbb0, enbb1]; ebufs = [ebuf0, ebuf1]; pbufs = [pbuf0, pbuf1]
        rbbs = [rbb0, rbb1]; stgs = [stg0, stg1]

        def nr(i):
            return nrow[0:1, L * i:L * (i + 1)]

        def njt(qb):
            return 4 * qb + 4

        def prev_jt(h, qb, jt):
            if jt >= 2:
                return h, qb, jt - 2
            if qb >= 1:
                return h, qb - 1, njt(qb - 1) - 2 + jt
            if h >= 1:
                return h - 1, NQB - 1, njt(NQB - 1) - 2 + jt
            return None

        def vbody(v):
            p.emit_consts(v)
            ve.do(lambda: v.memset(onescol[:], 1.0))
            ve.do(lambda: v.memset(onescol32[:], 1.0))
            m_cst[0] = ve.do(lambda: v.memset(ones1r[:], 1.0))
            for h in range(NH):
                par = h % 2
                ve.w(se, m_nke[h][7])
                ve.do(lambda: v.reduce_max(out=small[0:1, 0:1], in_=nr(1),
                                           axis=AX.X))
                ve.w(se, m_bnd[h] - 1)
                ve.do(lambda: v.tensor_scalar(
                    out=small[0:1, 2:3], in0=small[0:1, 1:2], scalar1=SC,
                    scalar2=None, op0=MULT))
                m_bndv[h] = ve.do(lambda: v.tensor_scalar(
                    out=nr(3), in0=nr(2), scalar1=small[0:1, 2:3],
                    scalar2=None, op0=MULT))
                for qb in range(NQB):
                    qpar = qb % 2
                    ve.w(se, m_enbev[h][qb])
                    for r in range(4):
                        m_masks[h][qb] = ve.do(lambda r=r, qpar=qpar:
                                               v.tensor_tensor(
                                                   out=enbm[:, r, :],
                                                   in0=enbbs[qpar][:],
                                                   in1=masks[:, r, :], op=MULT))
                    for jt in range(njt(qb)):
                        jpar = jt % 2
                        diag = jt - 4 * qb
                        src_enb = (enbm[:, diag, :] if diag >= 0
                                   else enbbs[qpar][:])
                        ve.w(se, m_exp[h][qb][jt])
                        pv = prev_jt(h, qb, jt)
                        if pv is not None:
                            ve.w(te, m_av_jt[pv[0]][pv[1]][pv[2]])
                        m_pt[h][qb][jt] = ve.do(
                            lambda jpar=jpar, src_enb=src_enb: v.tensor_tensor(
                                out=pbufs[jpar][:], in0=ebufs[jpar][:],
                                in1=src_enb, op=MULT))
                    ve.w(se, m_sev[h][qb])
                    m_rec[h][qb] = ve.do(lambda: v.reciprocal(out=rrow[:],
                                                              in_=srow[:]))
                    ve.w(se, m_rev[h][qb])
                    if h >= 2 and qb == 0:
                        ve.w(sa[par], 16 * (h // 2))
                    m_avev[h][qb] = ve.do(lambda par=par, qb=qb, qpar=qpar:
                                          v.tensor_tensor(
                                              out=stgs[par][:,
                                                            512 * qb:512 * qb + 512],
                                              in0=psAV[:, 512 * qpar:
                                                       512 * qpar + 512],
                                              in1=rbbs[qpar][:], op=MULT))

        m_av_jt = [[[0] * 36 for _ in range(NQB)] for _ in range(NH)]

        def sbody(sc):
            for h in range(NH):
                par = h % 2
                se.w(hl[par], 16 * 6 * (h // 2 + 1))
                if h >= 1:
                    se.w(te, m_hu[h - 1])
                m_q2[h] = se.do(lambda par=par: sc.activation(
                    out=q2[:], in_=qhs[par][:], func=AF.Square))
                for n in range(8):
                    se.w(te, m_nq[h][n])
                    m_nqe[h][n] = se.do(lambda n=n: sc.copy(
                        out=nr(0)[0:1, 512 * n:512 * n + 512],
                        in_=psMsc[0:1, 512 * (n % 2):512 * (n % 2) + 512]))
                m_k2[h] = se.do(lambda par=par: sc.activation(
                    out=q2[:], in_=khs[par][:], func=AF.Square))
                for n in range(8):
                    se.w(te, m_nk[h][n])
                    m_nke[h][n] = se.do(lambda n=n: sc.copy(
                        out=nr(1)[0:1, 512 * n:512 * n + 512],
                        in_=psMsc[0:1, 512 * (n % 2):512 * (n % 2) + 512]))
                se.do(lambda: sc.activation(out=nr(2), in_=nr(0), func=AF.Sqrt))
                ve_gate = m_bndv[h]
                se.w(ve, ve_gate - 2)
                m_bnd[h] = se.do(lambda: sc.activation(
                    out=small[0:1, 1:2], in_=small[0:1, 0:1], func=AF.Sqrt))
                se.w(ve, m_bndv[h])
                m_bnd[h] = se.do(lambda: sc.activation(
                    out=enbrow[:], in_=nr(3), func=AF.Exp, scale=-1.0))
                for qb in range(NQB):
                    qpar = qb % 2
                    se.w(te, m_enbmm[h][qb])
                    m_enbev[h][qb] = se.do(lambda qpar=qpar: sc.copy(
                        out=enbbs[qpar][:],
                        in_=psMsc[:, 512 * qpar:512 * qpar + 512]))
                    for jt in range(njt(qb)):
                        jpar = jt % 2
                        se.w(te, m_st[h][qb][jt])
                        pv = prev_jt(h, qb, jt)
                        if pv is not None:
                            se.w(ve, m_pt[pv[0]][pv[1]][pv[2]])
                        m_exp[h][qb][jt] = se.do(lambda jpar=jpar: sc.activation(
                            out=ebufs[jpar][:],
                            in_=psST[:, 512 * jpar:512 * jpar + 512],
                            func=AF.Exp, scale=SC))
                    se.w(te, m_sum[h][qb])
                    m_sev[h][qb] = se.do(lambda qpar=qpar: sc.copy(
                        out=srow[:], in_=psSum[0:1, 512 * qpar:512 * qpar + 512]))
                    se.w(te, m_rmm[h][qb])
                    m_rev[h][qb] = se.do(lambda qpar=qpar: sc.copy(
                        out=rbbs[qpar][:],
                        in_=psMsc[:, 512 * qpar:512 * qpar + 512]))

        def tbody(tn):
            te.w(cl, 16 * 2)
            te.w(ve, m_cst[0])
            for h in range(NH):
                par = h % 2
                te.w(hl[par], 16 * 6 * (h // 2 + 1))
                if h >= 1:
                    te.w(se, m_rev[h - 1][NQB - 1])
                te.w(se, m_q2[h])
                for n in range(8):
                    if n >= 2:
                        te.w(se, m_nqe[h][n - 2])
                    m_nq[h][n] = te.do(lambda n=n: tn.matmul(
                        psMsc[0:1, 512 * (n % 2):512 * (n % 2) + 512],
                        onescol32[:], q2[:, 512 * n:512 * n + 512],
                        start=True, stop=True), selfsync=False)
                te.w(se, m_k2[h])
                for n in range(8):
                    te.w(se, m_nqe[h][n] if n >= 6 else m_nqe[h][7])
                    if n >= 2:
                        te.w(se, m_nke[h][n - 2])
                    m_nk[h][n] = te.do(lambda n=n: tn.matmul(
                        psMsc[0:1, 512 * (n % 2):512 * (n % 2) + 512],
                        onescol32[:], q2[:, 512 * n:512 * n + 512],
                        start=True, stop=True), selfsync=False)
                for qb in range(NQB):
                    qpar = qb % 2
                    if qb >= 2:
                        te.w(ve, m_avev[h][qb - 2])
                        te.w(se, m_sev[h][qb - 2])
                    elif h >= 1:
                        te.w(ve, m_avev[h - 1][NQB - 1])
                        te.w(se, m_sev[h - 1][NQB - 1])
                    te.w(se, m_bnd[h])
                    if qb >= 2:
                        te.w(se, m_rev[h][qb - 2])
                    m_enbmm[h][qb] = te.do(lambda h=h, qb=qb, qpar=qpar:
                                           tn.matmul(
                                               psMsc[:, 512 * qpar:
                                                     512 * qpar + 512],
                                               ones1r[:],
                                               enbrow[0:1,
                                                      512 * qb:512 * qb + 512],
                                               start=True, stop=True),
                                           selfsync=False)
                    for jt in range(njt(qb)):
                        jpar = jt % 2
                        pv = prev_jt(h, qb, jt)
                        if pv is not None:
                            te.w(se, m_exp[pv[0]][pv[1]][pv[2]])
                        m_st[h][qb][jt] = te.do(lambda par=par, jt=jt, qb=qb,
                                                jpar=jpar: tn.matmul(
                            psST[:, 512 * jpar:512 * jpar + 512],
                            khs[par][:, 128 * jt:128 * (jt + 1)],
                            qhs[par][:, 512 * qb:512 * qb + 512],
                            start=True, stop=True), selfsync=False)
                        te.w(ve, m_pt[h][qb][jt])
                        m_sum[h][qb] = te.do(lambda jpar=jpar, qpar=qpar, jt=jt,
                                             qb=qb: tn.matmul(
                            psSum[0:1, 512 * qpar:512 * qpar + 512],
                            onescol[:], pbufs[jpar][:],
                            start=(jt == 0), stop=(jt == njt(qb) - 1)),
                            selfsync=False)
                        cnt = te.do(lambda par=par, jpar=jpar, qpar=qpar, jt=jt,
                                    qb=qb: tn.matmul(
                            psAV[:, 512 * qpar:512 * qpar + 512],
                            vbs[par][:, jt, :], pbufs[jpar][:],
                            start=(jt == 0), stop=(jt == njt(qb) - 1)),
                            selfsync=False)
                        m_av_jt[h][qb][jt] = cnt
                    m_av[h][qb] = cnt
                    te.w(ve, m_rec[h][qb])
                    te.w(se, m_enbev[h][qb])
                    m_rmm[h][qb] = te.do(lambda qpar=qpar: tn.matmul(
                        psMsc[:, 512 * qpar:512 * qpar + 512], ones1r[:],
                        rrow[:], start=True, stop=True), selfsync=False)
                m_hu[h] = te.n

        def syncbody(sync):
            p.dma(sync, masks[:],
                  t["masks"][:].rearrange("r pp n -> pp r n"), cl)
            p.dma(sync, enbm[:, 0:1, 0:1] if False else small[0:1, 0:8],
                  t["masks"][0, 0:1, 0:8], cl)
            for h in range(NH):
                par = h % 2
                if h >= 2:
                    p.qwait(sync, te, m_hu[h - 2])
                    p.qwait(sync, se, m_k2[h - 2])
                p.dma(sync, qhs[par][:], t["q_scr"][h], hl[par])
                p.dma(sync, khs[par][:], t["k_scr"][h], hl[par])
                for c in range(4):
                    p.dma(sync, vbs[par][:, 8 * c:8 * (c + 1), :],
                          t["v_scr"][8 * c:8 * (c + 1), :, h, :]
                          .rearrange("lt pp hd -> pp lt hd"), hl[par])
                p.qwait(sync, ve, m_avev[h][NQB - 1])
                p.dma(sync, t["av_scr"][h], stgs[par][:], sa[par])
            for q in range(2):
                p.qwait(sync, sa[q], sa[q].n)

        p.two_pass({"vector": vbody, "scalar": sbody, "tensor": tbody,
                    "sync": syncbody})
    return



def emit_p9pre(p):
    nc, t = p.nc, p.t
    se = p.se
    ldq = [p.sem("p9p_ld0"), p.sem("p9p_ld1")]
    stq = [p.sem("p9p_st0"), p.sem("p9p_st1")]
    jobs = []
    for w, (snm, dnm, KP) in enumerate([("gq_T", "l1g16", KC),
                                        ("uq_T", "l1u16", KC),
                                        ("dq_T", "l1d16", 64)]):
        ncols = 8192 if w < 2 else D
        for k in range(KP):
            for c in range(ncols // 2048):
                jobs.append((snm, dnm, k, c))
    m_cast = [0] * len(jobs)

    import contextlib as _ctl
    with _ctl.ExitStack() as _es:
        _sb = lambda n, sh, dt: _es.enter_context(nc.sbuf_tensor(n, sh, dt))
        f0 = _sb("p9p_f0", [128, 2048], F32)
        f1 = _sb("p9p_f1", [128, 2048], F32)
        b0 = _sb("p9p_b0", [128, 2048], BF16)
        b1 = _sb("p9p_b1", [128, 2048], BF16)
        fs = [f0, f1]; bs = [b0, b1]

        def sbody(sc):
            for j, (snm, dnm, k, c) in enumerate(jobs):
                par = j % 2
                se.w(ldq[par], 16 * (j // 2 + 1))
                if j >= 2:
                    se.w(stq[par], 16 * (j // 2))
                m_cast[j] = se.do(lambda par=par: sc.activation(
                    out=bs[par][:], in_=fs[par][:], func=AF.Copy, bias=-128.0))

        def gbody(g_):
            for j, (snm, dnm, k, c) in enumerate(jobs):
                par = j % 2
                if j >= 2:
                    p.qwait(g_, se, m_cast[j - 2])
                p.dma(g_, fs[par][:],
                      t[snm][128 * k:128 * (k + 1), 2048 * c:2048 * (c + 1)],
                      ldq[par])

        def syncbody(sync):
            for j, (snm, dnm, k, c) in enumerate(jobs):
                par = j % 2
                p.qwait(sync, se, m_cast[j])
                if dnm in ("l1g16", "l1u16"):
                    dst = t[dnm][16 * c:16 * (c + 1), :, k, :] \
                        .rearrange("i pp m -> pp i m")
                else:
                    dst = t[dnm][16 * c:16 * (c + 1), :, k, :] \
                        .rearrange("i pp m -> pp i m")
                p.dma(sync, dst, bs[par][:].rearrange("pp (i m) -> pp i m", m=128),
                      stq[par])
            for q in range(2):
                p.qwait(sync, stq[q], stq[q].n)

        p.two_pass({"scalar": sbody, "gpsimd": gbody, "sync": syncbody})
        p.sb["p9pre_done"] = [(stq[0], stq[0].n), (stq[1], stq[1].n)]
    return


def emit_p9(p):
    nc, t = p.nc, p.t
    ve, se, te = p.ve, p.se, p.te
    NLB = 4
    LB = L // NLB
    NC2 = LB // 512
    cl = p.sem("p9_cl")
    xl = [p.sem("p9_xl%d" % b) for b in range(NLB)]
    gwl = [p.sem("p9_gwl0"), p.sem("p9_gwl1")]
    dwl = [p.sem("p9_dwl0"), p.sem("p9_dwl1")]
    so = [p.sem("p9_so0"), p.sem("p9_so1")]
    NI = 64

    m_gmm = [[0] * NC2 for _ in range(NLB * NI)]
    m_umm = [[0] * NC2 for _ in range(NLB * NI)]
    m_sg = [[0] * NC2 for _ in range(NLB * NI)]
    m_h = [[0] * NC2 for _ in range(NLB * NI)]
    m_dmm = [[0] * NC2 for _ in range(NLB * 16)]
    m_dev = [[0] * NC2 for _ in range(NLB * 16)]
    m_guw = [0] * (NLB * NI)
    m_dw = [0] * (NLB * 16)
    m_xu = [0] * NLB
    m_hu = [0] * NLB

    import contextlib as _ctl
    with _ctl.ExitStack() as _es:
        _sb = lambda n, sh, dt: _es.enter_context(nc.sbuf_tensor(n, sh, dt))
        _ps = lambda n, sh, dt: _es.enter_context(nc.psum_tensor(n, sh, dt))
        xnb = _sb("p9_xnb", [128, KC, LB], BF16)
        gw0 = _sb("p9_gw0", [128, KC, 128], BF16)
        gw1 = _sb("p9_gw1", [128, KC, 128], BF16)
        uw0 = _sb("p9_uw0", [128, KC, 128], BF16)
        uw1 = _sb("p9_uw1", [128, KC, 128], BF16)
        dw0 = _sb("p9_dw0", [128, 64, 128], BF16)
        gsc = _sb("p9_gsc", [128, 64], F32)
        usc = _sb("p9_usc", [128, 64], F32)
        dsc = _sb("p9_dsc", [128, 16], F32)
        sg = _sb("p9_sg", [128, 1024], F32)
        hblk = _sb("p9_hblk", [128, NI, LB], BF16)
        stg0 = _sb("p9_stg0", [128, LB], BF16)
        stg1 = _sb("p9_stg1", [128, LB], BF16)
        psA = _ps("p9_psA", [128, 2048], F32)
        psB = _ps("p9_psB", [128, 2048], F32)
        gws = [gw0, gw1]; uws = [uw0, uw1]; dws = [dw0, dw0]
        stgs = [stg0, stg1]

        def gb(n):
            return psA[:, 512 * (n % 2):512 * (n % 2) + 512]

        def ub(n):
            return psA[:, 1024 + 512 * (n % 2):1024 + 512 * (n % 2) + 512]

        def db(n):
            return psB[:, 512 * (n % 2):512 * (n % 2) + 512]

        def tbody(tn):
            te.w(cl, 16 * 3)
            for lb in range(NLB):
                te.w(xl[lb], 16 * KC)
                for i in range(NI):
                    s = lb * NI + i
                    par = s % 2
                    te.w(gwl[par], 32 * (s // 2 + 1))
                    for n in range(NC2):
                        idx = s * NC2 + n
                        if idx >= 2:
                            pi = idx - 2
                            te.w(se, m_sg[pi // NC2][pi % NC2])
                        for k in range(KC):
                            cnt = te.do(lambda par=par, k=k, n=n: tn.matmul(
                                gb(n), gws[par][:, k, :],
                                xnb[:, k, 512 * n:512 * n + 512],
                                start=(k == 0), stop=(k == KC - 1)),
                                selfsync=False)
                        m_gmm[s][n] = cnt
                        if idx >= 2:
                            pi = idx - 2
                            te.w(ve, m_h[pi // NC2][pi % NC2])
                        for k in range(KC):
                            cnt = te.do(lambda par=par, k=k, n=n: tn.matmul(
                                ub(n), uws[par][:, k, :],
                                xnb[:, k, 512 * n:512 * n + 512],
                                start=(k == 0), stop=(k == KC - 1)),
                                selfsync=False)
                        m_umm[s][n] = cnt
                    m_guw[s] = te.n
                te.w(ve, m_h[lb * NI + NI - 1][NC2 - 1])
                for d in range(16):
                    sd = lb * 16 + d
                    dpar = 0
                    te.w(dwl[0], 16 * (sd + 1))
                    for n in range(NC2):
                        idx = sd * NC2 + n
                        if idx >= 2:
                            pi = idx - 2
                            te.w(ve, m_dev[pi // NC2][pi % NC2])
                        for kk in range(64):
                            cnt = te.do(lambda dpar=dpar, kk=kk, n=n, lb=lb:
                                        tn.matmul(
                                            db(n), dws[dpar][:, kk, :],
                                            hblk[:, kk, 512 * n:512 * n + 512],
                                            start=(kk == 0), stop=(kk == 63)),
                                        selfsync=False)
                        m_dmm[sd][n] = cnt
                    m_dw[sd] = te.n
                m_xu[lb] = te.n
                m_hu[lb] = te.n

        def sbody(sc):
            se.w(cl, 16 * 3)
            for lb in range(NLB):
                for i in range(NI):
                    s = lb * NI + i
                    for n in range(NC2):
                        se.w(te, m_gmm[s][n])
                        if s >= 1:
                            se.w(ve, m_h[s - 1][n])
                        m_sg[s][n] = se.do(lambda i=i, n=n: sc.activation(
                            out=sg[:, 512 * n:512 * n + 512], in_=gb(n),
                            func=AF.Silu, scale=gsc[:, i:i + 1]))

        def vbody(v):
            for lb in range(NLB):
                for i in range(NI):
                    s = lb * NI + i
                    for n in range(NC2):
                        ve.w(te, m_umm[s][n])
                        ve.w(se, m_sg[s][n])
                        if lb >= 1 and i == 0 and n == 0:
                            ve.w(te, m_hu[lb - 1])
                        m_h[s][n] = ve.do(lambda i=i, n=n: v.scalar_tensor_tensor(
                            out=hblk[:, i, 512 * n:512 * n + 512], in0=ub(n),
                            scalar=usc[:, i:i + 1],
                            in1=sg[:, 512 * n:512 * n + 512],
                            op0=MULT, op1=MULT))
                for d in range(16):
                    sd = lb * 16 + d
                    for n in range(NC2):
                        ve.w(te, m_dmm[sd][n])
                        if sd >= 2 and n == 0:
                            ve.w(so[sd % 2], 16 * (sd // 2))
                        m_dev[sd][n] = ve.do(lambda d=d, n=n, sd=sd:
                                             v.tensor_scalar(
                                                 out=stgs[sd % 2][:,
                                                                  512 * n:512 * n + 512],
                                                 in0=db(n),
                                                 scalar1=dsc[:, d:d + 1],
                                                 scalar2=None, op0=MULT))

        def gbody(g_):
            for sem_, cnt_ in p.sb.get("p9pre_done", []):
                p.qwait(g_, sem_, cnt_)
            for lb in range(NLB):
                for i in range(NI):
                    s = lb * NI + i
                    par = s % 2
                    if s >= 2:
                        p.qwait(g_, te, m_guw[s - 2])
                    p.dma(g_, gws[par][:], t["l1g16"][i], gwl[par])
                    p.dma(g_, uws[par][:], t["l1u16"][i], gwl[par])
                for d in range(16):
                    sd = lb * 16 + d
                    if sd >= 1:
                        p.qwait(g_, te, m_dw[sd - 1])
                    p.dma(g_, dws[0][:], t["l1d16"][d], dwl[0])

        def syncbody(sync):
            p.dma(sync, gsc[:], t["gsc_blk"][:], cl)
            p.dma(sync, usc[:], t["usc_blk"][:], cl)
            p.dma(sync, dsc[:], t["dsc_blk"][:], cl)
            for lb in range(NLB):
                if lb >= 1:
                    p.qwait(sync, te, m_xu[lb - 1])
                for k in range(KC):
                    p.dma(sync, xnb[:, k, :],
                          t["xn1_scr"][k, :, LB * lb:LB * (lb + 1)], xl[lb])
                for d in range(16):
                    sd = lb * 16 + d
                    p.qwait(sync, ve, m_dev[sd][NC2 - 1])
                    p.dma(sync, t["ffn1_scr"][d, :, LB * lb:LB * (lb + 1)],
                          stgs[sd % 2][:], so[sd % 2])
            for q in range(2):
                p.qwait(sync, so[q], so[q].n)

        p.two_pass({"vector": vbody, "scalar": sbody, "tensor": tbody,
                    "gpsimd": gbody, "sync": syncbody})
    return



def emit_p4pre(p):
    nc, t = p.nc, p.t
    se = p.se
    ldq = [p.sem("p4p_ld0"), p.sem("p4p_ld1")]
    stq = [p.sem("p4p_st0"), p.sem("p4p_st1")]
    jobs = []
    for e in range(8):
        for i in range(8):
            jobs.append(("g", e, i))
            jobs.append(("u", e, i))
    for e in range(8):
        for ii in range(8):
            jobs.append(("d", e, ii))
    m_done = [0] * len(jobs)

    import contextlib as _ctl
    with _ctl.ExitStack() as _es:
        _sb = lambda n, sh, dt: _es.enter_context(nc.sbuf_tensor(n, sh, dt))
        b0 = _sb("p4p_b0", [128, KC, 128], BF16)
        b1 = _sb("p4p_b1", [128, KC, 128], BF16)
        bs = [b0, b1]

        def gbody(g_):
            for j, (w, e, i) in enumerate(jobs):
                par = j % 2
                if j >= 2:
                    p.qwait(g_, stq[par], 16 * (j // 2))
                if w == "g":
                    srcap = t["moe_gate_T"][e][:, 128 * i:128 * (i + 1)] \
                        .rearrange("(k pp) m -> pp k m", pp=128)
                elif w == "u":
                    srcap = t["moe_up_T"][e][:, 128 * i:128 * (i + 1)] \
                        .rearrange("(k pp) m -> pp k m", pp=128)
                else:
                    srcap = t["moe_down_T"][e][128 * i:128 * (i + 1), :] \
                        .rearrange("pp (d m) -> pp d m", m=128)
                p.dma(g_, bs[par][:], srcap, ldq[par])

        def syncbody(sync):
            for j, (w, e, i) in enumerate(jobs):
                par = j % 2
                p.qwait(sync, ldq[par], 16 * (j // 2 + 1))
                if w == "g":
                    dst = t["moe_g16"][e, i]
                    p.dma(sync, dst, bs[par][:], stq[par])
                elif w == "u":
                    dst = t["moe_u16"][e, i]
                    p.dma(sync, dst, bs[par][:], stq[par])
                else:
                    dst = t["moe_d16"][:, :, 8 * e + i, :] \
                        .rearrange("d pp m -> pp d m")
                    p.dma(sync, dst, bs[par][:], stq[par])
            for q in range(2):
                p.qwait(sync, stq[q], stq[q].n)

        p.two_pass({"gpsimd": gbody, "sync": syncbody})
        p.sb["p4pre_done"] = [(stq[0], stq[0].n), (stq[1], stq[1].n)]
    return


def emit_p4(p):
    nc, t = p.nc, p.t
    ve, se, te = p.ve, p.se, p.te
    NLB = 8
    LBS = 512
    cl = p.sem("p4_cl")
    xr = [p.sem("p4_xr%d" % n) for n in range(8)]
    xl = [p.sem("p4_xl%d" % b) for b in range(NLB)]
    gwl = [p.sem("p4_gwl0"), p.sem("p4_gwl1")]
    dwl = p.sem("p4_dwl")
    so = [p.sem("p4_so0"), p.sem("p4_so1")]

    m_cmm = [0] * 8; m_cev = [0] * 8
    m_tr = [0] * 32; m_trev = [0] * 32
    m_rt = [0] * 32
    m_wt = [[0] * 13 for _ in range(32)]
    m_wtev = [[0] * 13 for _ in range(32)]
    m_ebb = [0]; m_ebbev = [0]; m_cst = [0]
    m_wbc = [[0] * 9 for _ in range(NLB)]
    m_wbcev = [[0] * 9 for _ in range(NLB)]
    m_gmm = [[0] * 64 for _ in range(NLB)]
    m_umm = [[0] * 64 for _ in range(NLB)]
    m_sg = [[0] * 64 for _ in range(NLB)]
    m_act = [[0] * 64 for _ in range(NLB)]
    m_dmm = [[0] * 16 for _ in range(NLB)]
    m_dev = [[0] * 16 for _ in range(NLB)]
    m_guw = [[0] * 64 for _ in range(NLB)]
    m_dw = [[0] * 16 for _ in range(NLB)]
    m_xu = [0] * NLB

    import contextlib as _ctl
    with _ctl.ExitStack() as _es:
        _sb = lambda n, sh, dt: _es.enter_context(nc.sbuf_tensor(n, sh, dt))
        _ps = lambda n, sh, dt: _es.enter_context(nc.psum_tensor(n, sh, dt))
        xnb = _sb("p4_xnb", [128, KC, LBS], BF16)
        cls16 = _sb("p4_cls", [128, KC, 12], BF16)
        id32 = _sb("p4_id", [128, 128], F32)
        ones1r = _sb("p4_o1r", [1, 128], F32)
        lgT = _sb("p4_lgT", [12, L], F32)
        lgsm = _sb("p4_lgsm", [128, 32, 12], F32)
        ebb = _sb("p4_ebb", [128, 12], F32)
        sm = _sb("p4_sm", [128, 8], F32)
        r0 = _sb("p4_r0", [128, 12], F32)
        r1 = _sb("p4_r1", [128, 12], F32)
        r2 = _sb("p4_r2", [128, 12], F32)
        r3 = _sb("p4_r3", [128, 12], F32)
        wsk = _sb("p4_wsk", [128, 32, 13], F32)
        wr = _sb("p4_wr", [1, 13 * 512], F32)
        wbb = _sb("p4_wbb", [128, 9, LBS], F32)
        gw0 = _sb("p4_gw0", [128, KC, 128], BF16)
        gw1 = _sb("p4_gw1", [128, KC, 128], BF16)
        uw0 = _sb("p4_uw0", [128, KC, 128], BF16)
        uw1 = _sb("p4_uw1", [128, KC, 128], BF16)
        dw = _sb("p4_dw", [128, 64, 128], BF16)
        sgb = _sb("p4_sgb", [128, LBS], F32)
        tf0 = _sb("p4_tf0", [128, LBS], F32)
        tf1 = _sb("p4_tf1", [128, LBS], F32)
        ablk = _sb("p4_ablk", [128, 64, LBS], BF16)
        stg0 = _sb("p4_stg0", [128, LBS], BF16)
        stg1 = _sb("p4_stg1", [128, LBS], BF16)
        psA = _ps("p4_psA", [128, 2048], F32)
        psB = _ps("p4_psB", [128, 2048], F32)
        gws = [gw0, gw1]; uws = [uw0, uw1]
        tfs = [tf0, tf1]; stgs = [stg0, stg1]

        def gb(par):
            return psA[:, 512 * par:512 * par + 512]

        def ub(par):
            return psA[:, 1024 + 512 * par:1024 + 512 * par + 512]

        def db(par):
            return psB[:, 512 * par:512 * par + 512]

        def wb_bank():
            return psB[:, 1024:1536]

        def rt_bank():
            return psB[:, 1536:2048]

        def vbody(v):
            p.emit_consts(v)
            m_cst[0] = ve.do(lambda: v.memset(ones1r[:], 1.0))
            for lt in range(32):
                ve.w(se, m_trev[lt])
                ve.w(se, m_ebbev[0])
                lg = lgsm[:, lt, :]
                ve.do(lambda lg=lg: v.reduce_max(out=sm[:, 0:1], in_=lg,
                                                 axis=AX.X, negate=True))
                ve.w(se, m_rt_exp[lt])
                ve.do(lambda: v.reduce_sum(out=sm[:, 1:2], in_=r0[:], axis=AX.X))
                ve.do(lambda: v.reciprocal(out=sm[:, 2:3], in_=sm[:, 1:2]))
                ve.do(lambda: v.tensor_scalar(out=r1[:], in0=r0[:],
                                              scalar1=sm[:, 2:3], scalar2=None,
                                              op0=MULT))
                ve.do(lambda: v.tensor_tensor(out=r2[:], in0=r1[:], in1=ebb[:],
                                              op=ADD))
                ve.do(lambda: v.reduce_max(out=sm[:, 3:4], in_=r2[:], axis=AX.X))
                ve.do(lambda: v.tensor_scalar(out=r0[:], in0=r2[:],
                                              scalar1=sm[:, 3:4], scalar2=None,
                                              op0=ISEQ))
                ve.do(lambda: v.scalar_tensor_tensor(out=r3[:], in0=r0[:],
                                                     scalar=-1e9, in1=r2[:],
                                                     op0=MULT, op1=ADD))
                ve.do(lambda: v.reduce_max(out=sm[:, 4:5], in_=r3[:], axis=AX.X))
                ve.do(lambda lt=lt: v.tensor_scalar(out=lgsm[:, lt, :], in0=r3[:],
                                                    scalar1=sm[:, 4:5],
                                                    scalar2=None, op0=ISEQ))
                ve.do(lambda lt=lt: v.tensor_tensor(out=r2[:], in0=r0[:],
                                                    in1=lgsm[:, lt, :], op=ADD))
                ve.do(lambda lt=lt: v.scalar_tensor_tensor(
                    out=wsk[:, lt, 0:12], in0=r2[:], scalar=1.5, in1=r1[:],
                    op0=MULT, op1=MULT))
                m_rt[lt] = ve.do(lambda lt=lt: v.reduce_sum(
                    out=wsk[:, lt, 12:13], in_=wsk[:, lt, 8:12], axis=AX.X))
            for lb in range(NLB):
                for e in range(8):
                    for i in range(8):
                        s = 8 * e + i
                        par = (lb * 64 + s) % 2
                        if lb >= 1 and s == 0:
                            ve.w(te, m_dmm[lb - 1][15])
                        ve.w(te, m_umm[lb][s])
                        ve.w(se, m_sg[lb][s])
                        ve.do(lambda par=par: v.tensor_tensor(
                            out=tfs[par][:], in0=sgb[:], in1=ub(par), op=MULT))
                        ve.w(se, m_wbcev[lb][e])
                        m_act[lb][s] = ve.do(lambda par=par, e=e, s=s, lb=lb:
                                             v.tensor_tensor(
                                                 out=ablk[:, s, :],
                                                 in0=tfs[par][:],
                                                 in1=wbb[:, e, :], op=MULT))
                for d in range(16):
                    par = (lb * 16 + d) % 2
                    ve.w(te, m_dmm[lb][d])
                    ve.w(se, m_wbcev[lb][8])
                    if lb * 16 + d >= 2:
                        pi = lb * 16 + d - 2
                        ve.w(so[pi % 2], 16 * (pi // 2 + 1))
                    ve.do(lambda d=d, par=par: v.tensor_tensor(
                        out=tfs[par][:], in0=xnb[:, d, :], in1=wbb[:, 8, :],
                        op=MULT))
                    m_dev[lb][d] = ve.do(lambda d=d, par=par: v.tensor_tensor(
                        out=stgs[(lb * 16 + d) % 2][:], in0=tfs[par][:],
                        in1=db(par), op=ADD))

        m_rt_exp = [0] * 32

        def sbody(sc):
            se.w(cl, 16 * 3)
            for n in range(8):
                se.w(te, m_cmm[n])
                m_cev[n] = se.do(lambda n=n: sc.copy(
                    out=lgT[0:12, 512 * n:512 * n + 512], in_=psA[0:12, 0:512]))
            se.w(te, m_ebb[0])
            m_ebbev[0] = se.do(lambda: sc.copy(out=ebb[:], in_=wb_bank()[:, 0:12]))
            for lt in range(32):
                se.w(te, m_tr[lt])
                m_trev[lt] = se.do(lambda lt=lt: sc.copy(
                    out=lgsm[:, lt, :], in_=rt_bank()[:, 0:12]))
            for lt in range(32):
                se.w(ve, m_rt[lt] - 12)
                if lt >= 1:
                    se.w(ve, m_rt[lt - 1])
                m_rt_exp[lt] = se.do(lambda lt=lt: sc.activation(
                    out=r0[:], in_=lgsm[:, lt, :], func=AF.Exp,
                    bias=sm[:, 0:1]))
            for lb in range(NLB):
                for lt4 in range(4):
                    lt = 4 * lb + lt4
                    for j in range(13):
                        se.w(te, m_wt[lt][j])
                        if lb >= 1:
                            se.w(te, m_wbc[lb - 1][8])
                        m_wtev[lt][j] = se.do(lambda lt4=lt4, j=j: sc.copy(
                            out=wr[0:1, 512 * j + 128 * lt4:
                                    512 * j + 128 * (lt4 + 1)],
                            in_=rt_bank()[0:1, 0:128]))
                for j in range(9):
                    se.w(te, m_wbc[lb][j])
                    m_wbcev[lb][j] = se.do(lambda lb=lb, j=j: sc.copy(
                        out=wbb[:, j, :], in_=wb_bank()))
                for e in range(8):
                    for i in range(8):
                        s = 8 * e + i
                        par = (lb * 64 + s) % 2
                        se.w(te, m_gmm[lb][s])
                        if lb * 64 + s >= 1:
                            ppi = lb * 64 + s - 1
                            se.w(ve, m_act[ppi // 64][ppi % 64])
                        m_sg[lb][s] = se.do(lambda par=par: sc.activation(
                            out=sgb[:], in_=gb(par), func=AF.Silu))

        def tbody(tn):
            te.w(cl, 16 * 3)
            te.w(ve, m_cst[0])
            for n in range(8):
                te.w(xr[n], 16 * KC)
                if n >= 1:
                    te.w(se, m_cev[n - 1])
                for k in range(KC):
                    cnt = te.do(lambda k=k: tn.matmul(
                        psA[0:12, 0:512], cls16[:, k, :], xnb[:, k, :],
                        start=(k == 0), stop=(k == KC - 1)), selfsync=False)
                m_cmm[n] = cnt
            m_ebb[0] = te.do(lambda: tn.matmul(
                wb_bank()[:, 0:12], ones1r[:], t_ebias_sb[0:1, :],
                start=True, stop=True), selfsync=False)
            for lt in range(32):
                te.w(se, m_cev[lt // 4])
                if lt >= 1:
                    te.w(se, m_trev[lt - 1])
                m_tr[lt] = te.do(lambda lt=lt: tn.matmul(
                    rt_bank()[:, 0:12], lgT[0:12, 128 * lt:128 * (lt + 1)],
                    id32[0:12, 0:12], start=True, stop=True), selfsync=False)
            for lb in range(NLB):
                te.w(xl[lb], 16 * KC)
                for lt4 in range(4):
                    lt = 4 * lb + lt4
                    te.w(ve, m_rt[lt])
                    if lb == 0 and lt4 == 0:
                        te.w(se, m_trev[31])
                    for j in range(13):
                        if lt + j >= 1:
                            pj = (lt, j - 1) if j >= 1 else (lt - 1, 12)
                            te.w(se, m_wtev[pj[0]][pj[1]])
                        m_wt[lt][j] = te.do(lambda lt=lt, j=j: tn.matmul(
                            rt_bank()[0:1, 0:128], wsk[:, lt, j:j + 1], id32[:],
                            start=True, stop=True), selfsync=False)
                for j in range(9):
                    if (lb, j) > (0, 0):
                        pj = (lb, j - 1) if j >= 1 else (lb - 1, 8)
                        te.w(se, m_wbcev[pj[0]][pj[1]])
                    te.w(se, m_wtev[4 * lb + 3][12])
                    if "refw" in P2OFF:
                        te.w(p.sems["p4_wdl"], 16 * (lb + 1))
                        m_wbc[lb][j] = te.do(lambda lb=lb, j=j: tn.matmul(
                            wb_bank(), ones1r[:],
                            wdbg_sb[0:1, 512 * j:512 * j + 512],
                            start=True, stop=True), selfsync=False)
                    else:
                        jj = j if j < 8 else 12
                        m_wbc[lb][j] = te.do(lambda jj=jj: tn.matmul(
                            wb_bank(), ones1r[:],
                            wr[0:1, 512 * jj:512 * jj + 512],
                            start=True, stop=True), selfsync=False)
                for e in range(8):
                    for i in range(8):
                        s = 8 * e + i
                        gs_ = lb * 64 + s
                        par = gs_ % 2
                        te.w(gwl[par], 32 * (gs_ // 2 + 1))
                        if gs_ >= 2:
                            te.w(se, m_sg[(gs_ - 2) // 64][(gs_ - 2) % 64])
                        for k in range(KC):
                            cnt = te.do(lambda par=par, k=k: tn.matmul(
                                gb(par), gws[par][:, k, :], xnb[:, k, :],
                                start=(k == 0), stop=(k == KC - 1)),
                                selfsync=False)
                        m_gmm[lb][s] = cnt
                        if gs_ >= 2:
                            te.w(ve, m_act[(gs_ - 2) // 64][(gs_ - 2) % 64])
                        for k in range(KC):
                            cnt = te.do(lambda par=par, k=k: tn.matmul(
                                ub(par), uws[par][:, k, :], xnb[:, k, :],
                                start=(k == 0), stop=(k == KC - 1)),
                                selfsync=False)
                        m_umm[lb][s] = cnt
                        m_guw[lb][s] = te.n
                te.w(ve, m_act[lb][63])
                for d in range(16):
                    gd = lb * 16 + d
                    par = gd % 2
                    te.w(dwl, 16 * (gd + 1))
                    if gd >= 2:
                        te.w(ve, m_dev[(gd - 2) // 16][(gd - 2) % 16])
                    for kk in range(64):
                        cnt = te.do(lambda par=par, kk=kk: tn.matmul(
                            db(par), dw[:, kk, :], ablk[:, kk, :],
                            start=(kk == 0), stop=(kk == 63)), selfsync=False)
                    m_dmm[lb][d] = cnt
                    m_dw[lb][d] = te.n
                m_xu[lb] = te.n

        def gbody(g_):
            p.dma(g_, cls16[:],
                  t["cls_T"][:].rearrange("(k pp) m -> pp k m", pp=128), cl)
            for sem_, cnt_ in p.sb.get("p4pre_done", []):
                p.qwait(g_, sem_, cnt_)
            for lb in range(NLB):
                for e in range(8):
                    for i in range(8):
                        s = 8 * e + i
                        gs_ = lb * 64 + s
                        par = gs_ % 2
                        if gs_ >= 2:
                            p.qwait(g_, te, m_guw[(gs_ - 2) // 64][(gs_ - 2) % 64])
                        p.dma(g_, gws[par][:], t["moe_g16"][e, i], gwl[par])
                        p.dma(g_, uws[par][:], t["moe_u16"][e, i], gwl[par])
                for d in range(16):
                    gd = lb * 16 + d
                    if gd >= 1:
                        p.qwait(g_, te, m_dw[(gd - 1) // 16][(gd - 1) % 16])
                    p.dma(g_, dw[:], t["moe_d16"][d], dwl)

        def syncbody(sync):
            p.dma(sync, id32[:], t["ident"][:], cl)
            p.dma(sync, t_ebias_sb[:], t["ebias"][:], cl)
            for n in range(8):
                for k in range(KC):
                    p.dma(sync, xnb[:, k, :],
                          t["xn_scr"][k, :, 512 * n:512 * n + 512], xr[n])
                p.qwait(sync, te, m_cmm[n])
            for lb in range(NLB):
                if lb == 0:
                    p.qwait(sync, te, m_cmm[7])
                else:
                    p.qwait(sync, te, m_xu[lb - 1])
                    p.qwait(sync, ve, m_dev[lb - 1][15])
                if "refw" in P2OFF:
                    if lb >= 1:
                        p.qwait(sync, te, m_wbc[lb - 1][8])
                    p.dma(sync, wdbg_sb[:],
                          t["wdbg"][lb].rearrange("a b -> (a b)")[None, :],
                          p.sems["p4_wdl"])
                for k in range(KC):
                    p.dma(sync, xnb[:, k, :],
                          t["xn_scr"][k, :, 512 * lb:512 * lb + 512], xl[lb])
                for d in range(16):
                    gd = lb * 16 + d
                    p.qwait(sync, ve, m_dev[lb][d])
                    p.dma(sync, t["ffn0_scr"][d, :, 512 * lb:512 * lb + 512],
                          stgs[gd % 2][:], so[gd % 2])
            if "dumpw" in P2OFF:
                for lt in range(32):
                    p.qwait(sync, ve, m_rt[lt])
                    p.dma(sync, t["wdump"][lt], wsk[:, lt, :], cl)
            if "dumpwr" in P2OFF:
                for lb in range(NLB):
                    p.qwait(sync, se, m_wtev[4 * lb + 3][12])
                    p.qwait(sync, te, m_wbc[lb][8])
                    p.dma(sync, t["wdbg"][lb].rearrange("a b -> (a b)")[None, :],
                          wr[0:1, 0:9 * 512], cl)
            for q in range(2):
                p.qwait(sync, so[q], so[q].n)

        t_ebias_sb = _sb("p4_ebrow", [1, 12], F32)
        wdbg_sb = _sb("p4_wdbg", [1, 9 * 512], F32) if "refw" in P2OFF else None
        if "refw" in P2OFF:
            wdl = p.sem("p4_wdl")
        p.two_pass({"vector": vbody, "scalar": sbody, "tensor": tbody,
                    "gpsimd": gbody, "sync": syncbody})
    return



def _mkprog(names, emits, **kw):
    nc = bass.Bass("TRN2", target_bir_lowering=False)
    ctx = contextlib.ExitStack()
    t = declare_dram(nc, names, **kw)
    with ctx:
        block = ctx.enter_context(nc.Block())
        p = P(nc, ctx, block)
        p.t = t
        for e in emits:
            e(p)
    return nc


def build_p0():
    return _mkprog(["x_in", "ident", "xT_scr", "xn_scr"], [emit_p0],
                   ext_scratch_out=("xT_scr", "xn_scr"))


P2_INS = ["xn_scr", "w_in_T", "w_gate_T", "conv_w_blk", "conv_b_blk",
          "C_blk", "D_blk", "B_blk_flat", "dtw_blk", "dt_b2", "Alog_pp", "bc2"]


def build_p2():
    return _mkprog(P2_INS + ["comb_scr"], [emit_p2],
                   ext_scratch_in=("xn_scr",), ext_scratch_out=("comb_scr",))


def build_p3():
    return _mkprog(["w_out_T", "comb_scr", "attn0_scr"], [emit_p3],
                   ext_scratch_in=("comb_scr",), ext_scratch_out=("attn0_scr",))


def build_p5():
    return _mkprog(["xT_scr", "attn0_scr", "ffn0_scr", "x1_scr", "xn1_scr"],
                   [emit_p5],
                   ext_scratch_in=("xT_scr", "attn0_scr", "ffn0_scr"),
                   ext_scratch_out=("x1_scr", "xn1_scr"))


def build_p6():
    return _mkprog(["xn1_scr", "Wq_T", "Wk_T", "Wv_T", "rope_cos", "rope_sin",
                    "pswap", "q_scr", "k_scr", "v_scr"], [emit_p6],
                   ext_scratch_in=("xn1_scr",),
                   ext_scratch_out=("q_scr", "k_scr", "v_scr"))


def build_p7():
    return _mkprog(["q_scr", "k_scr", "v_scr", "masks", "av_scr"], [emit_p7],
                   ext_scratch_in=("q_scr", "k_scr", "v_scr"),
                   ext_scratch_out=("av_scr",))


def build_p8():
    return _mkprog(["Wo_T", "av_scr", "o1_scr"],
                   [lambda p: emit_mm_ft(p, "p8", "Wo_T", "av_scr", "o1_scr",
                                         D, D)],
                   ext_scratch_in=("av_scr",), ext_scratch_out=("o1_scr",))


def build_p4():
    ins = ["xn_scr", "cls_T", "ebias", "ident", "moe_gate_T",
           "moe_up_T", "moe_down_T", "moe_g16", "moe_u16", "moe_d16",
           "ffn0_scr"]
    exi = ["xn_scr"]
    if "refw" in P2OFF:
        ins.append("wdbg")
        exi.append("wdbg")
    if "dumpwr" in P2OFF:
        ins.append("wdbg")
        exo0 = True
    exo = ["ffn0_scr"]
    if "dumpw" in P2OFF:
        ins.append("wdump")
        exo.append("wdump")
    if "dumpwr" in P2OFF:
        exo.append("wdbg")
    return _mkprog(ins, [emit_p4pre, emit_p4],
                   ext_scratch_in=tuple(exi), ext_scratch_out=tuple(exo))


def build_p9():
    return _mkprog(["gq_T", "uq_T", "dq_T", "gsc_blk", "usc_blk", "dsc_blk",
                    "xn1_scr", "l1g16", "l1u16", "l1d16", "ffn1_scr"],
                   [emit_p9pre, emit_p9],
                   ext_scratch_in=("xn1_scr",), ext_scratch_out=("ffn1_scr",))


def build_p10():
    return _mkprog(["ident", "x1_scr", "o1_scr", "ffn1_scr", "y_out"],
                   [emit_p10],
                   ext_scratch_in=("x1_scr", "o1_scr", "ffn1_scr"))



P5_INS = ["xT_scr", "attn0_scr", "ffn0_scr"]


def kernel(**inputs):
    hp = host_prep(inputs)
    x = np.asarray(inputs["L0_attn_hidden_states"])
    ncore = 8

    def feed(names, extra):
        maps = []
        for c in range(ncore):
            m = {}
            for nm in names:
                if nm == "x_in":
                    m[nm] = np.ascontiguousarray(x[c])
                elif nm in hp:
                    m[nm] = hp[nm]
                else:
                    m[nm] = extra[c][nm]
            maps.append(m)
        return maps

    def run(build, names, extra):
        nc = build()
        res = run_bass_kernel_spmd(nc, feed(names, extra), core_ids=list(range(ncore)))
        return res.results

    ex = [dict() for _ in range(ncore)]

    def absorb(results):
        for c in range(ncore):
            for k, v in results[c].items():
                ex[c][k] = v

    absorb(run(build_p0, ["x_in", "ident"], ex))
    absorb(run(build_p2, P2_INS, ex))
    absorb(run(build_p3, ["w_out_T", "comb_scr"], ex))
    absorb(run(build_p4, ["xn_scr", "cls_T", "ebias", "ident", "moe_gate_T",
                          "moe_up_T", "moe_down_T"], ex))
    absorb(run(build_p5, ["xT_scr", "attn0_scr", "ffn0_scr"], ex))
    absorb(run(build_p6, ["xn1_scr", "Wq_T", "Wk_T", "Wv_T", "rope_cos",
                          "rope_sin", "pswap"], ex))
    absorb(run(build_p7, ["q_scr", "k_scr", "v_scr", "masks"], ex))
    absorb(run(build_p8, ["Wo_T", "av_scr"], ex))
    absorb(run(build_p9, ["gq_T", "uq_T", "dq_T", "gsc_blk", "usc_blk",
                          "dsc_blk", "xn1_scr"], ex))
    absorb(run(build_p10, ["ident", "x1_scr", "o1_scr", "ffn1_scr"], ex))
    out = np.stack([np.asarray(ex[c]["y_out"], np.float32) for c in range(ncore)])
    return out
